# version 17
# baseline (speedup 1.0000x reference)
import numpy as np
from contextlib import ExitStack

import concourse.bass as bass
import concourse.mybir as mybir
from concourse.bass_utils import run_bass_kernel_spmd
from concourse.aot_env import allow_ici_all_to_all
import ml_dtypes

F32 = mybir.dt.float32
BF16 = mybir.dt.bfloat16
FP16 = mybir.dt.float16
I8 = mybir.dt.int8
I32 = mybir.dt.int32
AF = mybir.ActivationFunctionType
ALU = mybir.AluOpType
AXX = mybir.AxisListType.X
NPBF = ml_dtypes.bfloat16

L = 8192
CDIM = 2048
H = 16
D = 128
NCORE = 8
HPC = H // NCORE
TOWN = L // NCORE
NT = TOWN // 128
NF = CDIM // 128
NS = L // 512
NEG_BIG = -1.0e30
M_CLAMP = -80.0

ENGS = ("pe", "act", "dve", "pool", "sp")
DEBUG = False


class Buf:
    __slots__ = ("name", "last_write", "readers")

    def __init__(self, name):
        self.name = name
        self.last_write = None
        self.readers = []


class Op:
    __slots__ = ("eng", "build", "needs_inc", "waits", "is_dma", "inc_amt", "val",
                 "sem_key", "is_cc", "dma_buf")

    def __init__(self, eng, build, is_dma):
        self.eng = eng
        self.build = build
        self.needs_inc = is_dma
        self.waits = []
        self.is_dma = is_dma
        self.inc_amt = 16 if is_dma else 1
        self.val = None
        self.sem_key = None
        self.is_cc = False
        self.dma_buf = None


class Plan:

    def __init__(self):
        self.ops = {e: [] for e in ENGS}
        self.cc_ops = []

    def _dep(self, op, dep_op):
        if dep_op is None:
            return
        if dep_op is op:
            return
        if (dep_op.eng == op.eng == "pe") and (not dep_op.is_dma) and (not op.is_dma):
            return
        dep_op.needs_inc = True
        op.waits.append(dep_op)

    def op(self, eng, build, reads=(), writes=(), is_dma=False):
        o = Op(eng, build, is_dma)
        for b in reads:
            self._dep(o, b.last_write)
        for b in writes:
            self._dep(o, b.last_write)
            for r in b.readers:
                self._dep(o, r)
        self.ops[eng].append(o)
        for b in writes:
            b.last_write = o
            b.readers = []
        for b in reads:
            b.readers.append(o)
        return o

    def dma(self, queue, build, reads=(), writes=()):
        o = self.op(queue, build, reads=reads, writes=writes, is_dma=True)
        o.dma_buf = writes[0] if writes else None
        return o

    def barrier(self):
        tails = {}
        for e in ENGS:
            for o in reversed(self.ops[e]):
                if not o.is_cc and o.build is not None:
                    tails[e] = o
                    break
        for e in ENGS:
            o = Op(e, None, False)
            for e2, t in tails.items():
                if e2 != e:
                    t.needs_inc = True
                    o.waits.append(t)
            self.ops[e].append(o)

    def cc_barrier(self, ccop):
        ccop.needs_inc = True
        for e in ENGS:
            o = Op(e, None, False)
            o.waits.append(ccop)
            self.ops[e].append(o)

    def materialize(self, nc, ctx):
        sems = {}
        for e in ENGS:
            sems[e] = ctx.enter_context(nc.semaphore("s_" + e))
            sems[e + "_dma"] = ctx.enter_context(nc.semaphore("s_" + e + "_dma"))
        sems["cc"] = ctx.enter_context(nc.semaphore("s_cc"))
        buf_sem_cnt = {}
        for e in ENGS:
            for o in self.ops[e]:
                if o.is_dma and o.dma_buf is not None:
                    key = "db_" + o.dma_buf.name
                    if key not in sems:
                        sems[key] = ctx.enter_context(nc.semaphore(key))
                        buf_sem_cnt[key] = 0
        for e in ENGS:
            c_cnt = 0
            d_cnt = 0
            for o in self.ops[e]:
                if getattr(o, "is_cc", False):
                    continue
                if o.needs_inc:
                    if o.is_dma:
                        if o.dma_buf is not None:
                            key = "db_" + o.dma_buf.name
                            buf_sem_cnt[key] += 16
                            o.val = buf_sem_cnt[key]
                            o.sem_key = key
                        else:
                            d_cnt += 16
                            o.val = d_cnt
                            o.sem_key = e + "_dma"
                    else:
                        c_cnt += 1
                        o.val = c_cnt
                        o.sem_key = e
        cc_cnt = 0
        for o in self.cc_ops:
            cc_cnt += 1
            o.val = cc_cnt
            o.sem_key = "cc"

        def runner(eng_name):
            def run(E):
                seen = {}
                for o in self.ops[eng_name]:
                    want = {}
                    for dep in o.waits:
                        k = dep.sem_key
                        v = dep.val
                        if v is None:
                            raise RuntimeError("dep without milestone")
                        if want.get(k, -1) < v:
                            want[k] = v
                    for k, v in sorted(want.items()):
                        if seen.get(k, -1) >= v:
                            continue
                        E.wait_ge(sems[k], v)
                        seen[k] = v
                    if o.build is None:
                        continue
                    instr = o.build(E)
                    if o.needs_inc:
                        instr.then_inc(sems[o.sem_key], o.inc_amt)
            return run
        return sems, runner


def _rope_tables():
    inv_freq = (1.0 / (10000.0 ** (np.arange(0, D, 2, dtype=np.float32) / np.float32(D)))).astype(np.float32)
    theta = (np.arange(H, dtype=np.float32)[:, None] * inv_freq[None, :]).astype(np.float32)
    return np.cos(theta).astype(np.float32), np.sin(theta).astype(np.float32)


def _m_blocks(core):
    cos, sin = _rope_tables()
    scale = np.float32(1.0) / np.sqrt(np.float32(D)).astype(np.float32)
    out = np.zeros((6, D, D), dtype=np.float32)
    for j, (kind, h) in enumerate([("q", 2 * core), ("q", 2 * core + 1),
                                   ("k", 2 * core), ("k", 2 * core + 1),
                                   ("v", 0), ("v", 0)]):
        if kind == "v":
            out[j] = np.eye(D, dtype=np.float32)
            continue
        m = np.zeros((D, D), dtype=np.float32)
        for i in range(D // 2):
            c = cos[h, i]; s = sin[h, i]
            m[2 * i, 2 * i] = c
            m[2 * i + 1, 2 * i] = -s
            m[2 * i, 2 * i + 1] = s
            m[2 * i + 1, 2 * i + 1] = c
        if kind == "q":
            m = (m * scale).astype(np.float32)
        out[j] = m
    return out.reshape(6 * D, D)


def build_nc():
    nc = bass.Bass("TRN2", target_bir_lowering=False)

    x_own = nc.dram_tensor("x_own", [TOWN, CDIM], F32, kind="ExternalInput")
    w_qkv = nc.dram_tensor("w_qkv", [6 * D, CDIM], F32, kind="ExternalInput")
    mrot = nc.dram_tensor("mrot", [6 * D, D], F32, kind="ExternalInput")
    wprojT = nc.dram_tensor("wprojT", [CDIM, CDIM], F32, kind="ExternalInput")
    wffnT = nc.dram_tensor("wffnT", [CDIM, CDIM], I8, kind="ExternalInput")
    wscale_r = nc.dram_tensor("wscale_r", [1, CDIM], F32, kind="ExternalInput")
    bias_r = nc.dram_tensor("bias_r", [1, CDIM], F32, kind="ExternalInput")
    out_c = nc.dram_tensor("out_c", [TOWN, CDIM], F32, kind="ExternalOutput")

    maskA_np = np.where(np.arange(D)[None, :] <= np.arange(D)[:, None], 0.0, NEG_BIG).astype(np.float32)
    maskB_np = maskA_np.T.copy()
    maskA_t = nc.inline_tensor(maskA_np, "maskA_c")
    maskB_t = nc.inline_tensor(maskB_np, "maskB_c")
    identf_t = nc.inline_tensor(np.eye(D, dtype=np.float32), "identf_c")
    identb_t = nc.inline_tensor(np.eye(D, dtype=np.float32).astype(NPBF), "identb_c")

    if DEBUG:
        dbg_xng = nc.dram_tensor("dbg_xng", [NCORE * CDIM, TOWN], BF16, kind="ExternalOutput")
        dbg_q = nc.dram_tensor("dbg_q", [128, HPC * L], BF16, kind="ExternalOutput")
        dbg_k = nc.dram_tensor("dbg_k", [128, HPC * L], BF16, kind="ExternalOutput")
        dbg_v = nc.dram_tensor("dbg_v", [128, HPC * L], BF16, kind="ExternalOutput")
        dbg_g = nc.dram_tensor("dbg_g", [TOWN, CDIM], FP16, kind="ExternalOutput")
        dbg_ai = nc.dram_tensor("dbg_ai", [CDIM, TOWN], BF16, kind="ExternalOutput")
        dbg_ao = nc.dram_tensor("dbg_ao", [CDIM, TOWN], BF16, kind="ExternalOutput")
        dbg_xnin = nc.dram_tensor("dbg_xnin", [CDIM, TOWN], BF16, kind="ExternalOutput")
        dbg_acc = nc.dram_tensor("dbg_acc", [128, CDIM], F32, kind="ExternalOutput")
        dbg_u2 = nc.dram_tensor("dbg_u2", [128, CDIM], F32, kind="ExternalOutput")
        dbg_wsb = nc.dram_tensor("dbg_wsb", [128, CDIM], F32, kind="ExternalOutput")
        dbg_biasb = nc.dram_tensor("dbg_biasb", [128, CDIM], F32, kind="ExternalOutput")
        dbg_wfb = nc.dram_tensor("dbg_wfb", [128, CDIM], BF16, kind="ExternalOutput")
        dbg_ln = nc.dram_tensor("dbg_ln", [128, 8], F32, kind="ExternalOutput")
        dbg_xnf = nc.dram_tensor("dbg_xnf", [128, CDIM], F32, kind="ExternalOutput")

    cdum_in = nc.dram_tensor("cdum_in", [128, 128], F32)
    cdum_out = nc.dram_tensor("cdum_out", [128, 128], F32)

    xn_in = [nc.dram_tensor(f"xn_in{i}", [CDIM, TOWN // 2], BF16) for i in range(2)]
    xn_gath = [nc.dram_tensor(f"xn_gath{i}", [NCORE * CDIM, TOWN // 2], BF16) for i in range(2)]
    g_sp = nc.dram_tensor("g_sp", [TOWN, CDIM], FP16)
    a2a_in = nc.dram_tensor("a2a_in", [CDIM, TOWN], BF16)
    a2a_out = nc.dram_tensor("a2a_out", [CDIM, TOWN], BF16)

    P = Plan()

    with ExitStack() as ctx:
        en = ctx.enter_context

        iscale = en(nc.sbuf_tensor("iscale", [128, NT], F32))
        maskA = en(nc.sbuf_tensor("maskA", [128, 128], F32))
        maskB = en(nc.sbuf_tensor("maskB", [128, 128], F32))
        identf = en(nc.sbuf_tensor("identf", [128, 128], F32))
        identb = en(nc.sbuf_tensor("identb", [128, 128], BF16))
        onesr = en(nc.sbuf_tensor("onesr", [1, 128], F32))
        onesrb = en(nc.sbuf_tensor("onesrb", [1, 128], BF16))
        onescb = en(nc.sbuf_tensor("onescb", [128, 1], BF16))

        b_qfm = Buf("q_fm"); b_kfm = Buf("k_fm"); b_vtm = Buf("v_tm")
        b_iscale = Buf("iscale")
        b_consts = Buf("consts")
        b_xn_in = [Buf("xn_in0"), Buf("xn_in1")]
        b_xn_gath = [Buf("xn_gath0"), Buf("xn_gath1")]
        b_g_sp = Buf("g_sp")
        b_a2a_in = Buf("a2a_in"); b_a2a_out = Buf("a2a_out")
        b_out = Buf("out_c")

        for dst, src in ((maskA, maskA_t), (maskB, maskB_t), (identf, identf_t), (identb, identb_t)):
            P.dma("sp", (lambda d, s: lambda E: E.dma_start(out=d[:, :], in_=s.ap()))(dst, src),
                  writes=[b_consts])
        P.op("dve", lambda E: E.memset(onesr[:, :], 1.0), writes=[b_consts])
        P.op("dve", lambda E: E.memset(onesrb[:, :], 1.0), writes=[b_consts])
        P.op("dve", lambda E: E.memset(onescb[:, :], 1.0), writes=[b_consts])

        b_cdum = Buf("cdum")
        P.dma("sp", lambda E: E.dma_start(out=cdum_in.ap(), in_=identf_t.ap()),
              writes=[b_cdum])

        def merged_emit_generic(l1, l2):
            n1, n2 = len(l1), len(l2)
            i1 = i2 = 0
            while i1 < n1 or i2 < n2:
                if i2 < n2 and (i1 >= n1 or i1 * n2 >= i2 * n1):
                    l2[i2](); i2 += 1
                else:
                    l1[i1](); i1 += 1

        def emit_flush_collective():
            o2 = P.op("pool", lambda E: E.collective_compute(
                "AllReduce", ALU.add, replica_groups=[list(range(NCORE))],
                ins=[cdum_in.ap().opt()], outs=[cdum_out.ap().opt()]),
                reads=[b_cdum], writes=[])
            o2.is_cc = True
            P.cc_ops.append(o2)
            P.cc_barrier(o2)

        with ExitStack() as p01:
            xt = [p01.enter_context(nc.sbuf_tensor(f"xt{i}", [128, CDIM], F32)) for i in range(2)]
            xnf = [p01.enter_context(nc.sbuf_tensor(f"xnf{i}", [128, CDIM], F32)) for i in range(2)]
            xnb = [p01.enter_context(nc.sbuf_tensor(f"xnb{i}", [128, CDIM], BF16)) for i in range(2)]
            qi32 = [p01.enter_context(nc.sbuf_tensor(f"qi32_{i}", [128, CDIM], I32)) for i in range(2)]
            xqb = [p01.enter_context(nc.sbuf_tensor(f"xqb{i}", [128, CDIM], BF16)) for i in range(2)]
            srow = [p01.enter_context(nc.sbuf_tensor(f"srow{i}", [128, 1], F32)) for i in range(2)]
            nmean = [p01.enter_context(nc.sbuf_tensor(f"nmean{i}", [128, 1], F32)) for i in range(2)]
            sq = [p01.enter_context(nc.sbuf_tensor(f"sq{i}", [128, CDIM], F32)) for i in range(2)]
            varr = [p01.enter_context(nc.sbuf_tensor(f"varr{i}", [128, 1], F32)) for i in range(2)]
            stdr = [p01.enter_context(nc.sbuf_tensor(f"stdr{i}", [128, 1], F32)) for i in range(2)]
            rstd = [p01.enter_context(nc.sbuf_tensor(f"rstd{i}", [128, 1], F32)) for i in range(2)]
            nmrs = [p01.enter_context(nc.sbuf_tensor(f"nmrs{i}", [128, 1], F32)) for i in range(2)]
            amax = [p01.enter_context(nc.sbuf_tensor(f"amax{i}", [128, 1], F32)) for i in range(2)]
            rin = [p01.enter_context(nc.sbuf_tensor(f"rin{i}", [128, 1], F32)) for i in range(2)]
            trp = [p01.enter_context(nc.psum_tensor(f"trp{i}", [128, 1024], BF16)) for i in range(2)]
            tstage = [p01.enter_context(nc.sbuf_tensor(f"tstage{i}", [128, 1024], BF16)) for i in range(2)]
            xqf = p01.enter_context(nc.sbuf_tensor("xqf", [128, NT * CDIM], BF16))
            wfb = [p01.enter_context(nc.sbuf_tensor(f"wfb{i}", [128, CDIM], BF16)) for i in range(NF)]
            wf8 = [p01.enter_context(nc.sbuf_tensor(f"wf8{i}", [128, CDIM], I8)) for i in range(2)]
            wsb = p01.enter_context(nc.sbuf_tensor("wsb", [128, CDIM], BF16))
            biasb = p01.enter_context(nc.sbuf_tensor("biasb", [128, CDIM], BF16))
            u1 = p01.enter_context(nc.sbuf_tensor("u1", [128, 512], F32))
            u2 = p01.enter_context(nc.sbuf_tensor("u2", [128, 512], F32))
            g16 = p01.enter_context(nc.sbuf_tensor("g16", [128, 512], FP16))
            gst = [p01.enter_context(nc.sbuf_tensor("gst0", [128, CDIM], FP16))] * 2
            bc_ps = p01.enter_context(nc.psum_tensor("bc_ps", [128, 512], F32))
            ffn_ps = [p01.enter_context(nc.psum_tensor(f"ffn_ps{i}", [128, 512], F32)) for i in range(4)]

            b_xt = [Buf("xt0"), Buf("xt1")]
            b_xnb = [Buf("xnb0"), Buf("xnb1")]
            b_xqb = [Buf("xqb0"), Buf("xqb1")]
            b_scr2 = [Buf("p0_scratch0"), Buf("p0_scratch1")]
            b_trp = [Buf("trp0"), Buf("trp1")]
            b_tst = [Buf("tstage0"), Buf("tstage1")]
            b_xqf = [Buf(f"xqf{i}") for i in range(NT)]
            b_wfb = [Buf(f"wfb{i}") for i in range(NF)]
            b_wf8 = [Buf("wf8_0"), Buf("wf8_1")]
            b_rows = Buf("rows")
            b_bcast = Buf("bcast")
            b_bcps = Buf("bc_ps")
            b_u = Buf("ffn_u")
            b_ffnps = [Buf(f"ffn_ps{i}") for i in range(4)]
            _bg = Buf("gst0")
            b_gst = [_bg, _bg]

            def prep_thunks():
                th = []
                for fc in range(NF):
                    def w(fc=fc):
                        r = fc % 2
                        P.dma("pool", lambda E: E.dma_start(
                            out=wf8[r][:, :], in_=wffnT.ap()[fc * 128:(fc + 1) * 128, :]),
                            writes=[b_wf8[r]])
                        P.op("dve", lambda E: E.tensor_copy(wfb[fc][:, :], wf8[r][:, :]),
                             reads=[b_wf8[r]], writes=[b_wfb[fc]])
                    th.append(w)
                return th

            def bc():
                for j in range(4):
                    P.op("pe", (lambda j: lambda E: E.matmul(bc_ps[:, :], onesr[:, :],
                                sq[0][0:1, j * 512:(j + 1) * 512], start=True, stop=True))(j),
                         reads=[b_scr2[0], b_consts], writes=[b_bcps])
                    P.op("act", (lambda j: lambda E: E.copy(out=wsb[:, j * 512:(j + 1) * 512],
                                in_=bc_ps[:, :]))(j), reads=[b_bcps], writes=[b_bcast])
                for j in range(4):
                    P.op("pe", (lambda j: lambda E: E.matmul(bc_ps[:, :], onesr[:, :],
                                sq[1][0:1, j * 512:(j + 1) * 512], start=True, stop=True))(j),
                         reads=[b_scr2[1], b_consts], writes=[b_bcps])
                    P.op("act", (lambda j: lambda E: E.copy(out=biasb[:, j * 512:(j + 1) * 512],
                                in_=bc_ps[:, :]))(j), reads=[b_bcps], writes=[b_bcast])

            def row_prep():
                P.dma("sp", lambda E: E.dma_start(out=sq[0][0:1, :], in_=wscale_r.ap()),
                      writes=[b_scr2[0]])
                P.dma("sp", lambda E: E.dma_start(out=sq[1][0:1, :], in_=bias_r.ap()),
                      writes=[b_scr2[1]])
                bc()

            xn_r = [xn_in[i].ap().rearrange("(f p) t -> p f t", p=128) for i in range(2)]

            def p0_thunks(tc):
                r = tc % 2
                bx = b_xt[r]
                th = []
                th.append(lambda: P.dma("sp", lambda E: E.dma_start(
                    out=xt[r][:, :], in_=x_own.ap()[tc * 128:(tc + 1) * 128, :]), writes=[bx]))
                bs = b_scr2[r]
                def ln1():
                    P.op("dve", lambda E: E.reduce_sum(srow[r][:, :], xt[r][:, :], axis=AXX, negate=True),
                         reads=[bx], writes=[bs])
                    P.op("dve", lambda E: E.tensor_scalar_mul(nmean[r][:, :], srow[r][:, :], 1.0 / CDIM),
                         reads=[bs], writes=[bs])
                    P.op("act", lambda E: E.activation(out=sq[r][:, :], in_=xt[r][:, :], func=AF.Square,
                                                       bias=nmean[r][:, :], scale=1.0),
                         reads=[bx, bs], writes=[bs])
                th.append(ln1)
                def ln2():
                    P.op("dve", lambda E: E.reduce_sum(varr[r][:, :], sq[r][:, :], axis=AXX),
                         reads=[bs], writes=[bs])
                    P.op("dve", lambda E: E.tensor_scalar(varr[r][:, :], varr[r][:, :], 1.0 / CDIM, 1e-5, ALU.mult, ALU.add),
                         reads=[bs], writes=[bs])
                    P.op("act", lambda E: E.activation(out=stdr[r][:, :], in_=varr[r][:, :], func=AF.Sqrt),
                         reads=[bs], writes=[bs])
                    P.op("dve", lambda E: E.reciprocal(rstd[r][:, :], stdr[r][:, :]),
                         reads=[bs], writes=[bs])
                    P.op("dve", lambda E: E.tensor_mul(nmrs[r][:, :], nmean[r][:, :], rstd[r][:, :]),
                         reads=[bs], writes=[bs])
                    P.op("dve", lambda E: E.tensor_scalar(xnf[r][:, :], xt[r][:, :], rstd[r][:, :],
                                                          nmrs[r][:, :], ALU.mult, ALU.add),
                         reads=[bx, bs], writes=[bs])
                    P.op("act", lambda E: E.copy(out=xnb[r][:, :], in_=xnf[r][:, :]),
                         reads=[bs], writes=[b_xnb[r]])
                th.append(ln2)
                def qn():
                    P.op("dve", lambda E: E.reduce_max(amax[r][:, :], xnf[r][:, :], axis=AXX, apply_absolute_value=True),
                         reads=[bs], writes=[bs])
                    P.op("dve", lambda E: E.tensor_scalar_max(amax[r][:, :], amax[r][:, :], 1e-7),
                         reads=[bs], writes=[bs])
                    P.op("dve", lambda E: E.tensor_scalar_mul(iscale[:, tc:tc + 1], amax[r][:, :], 1.0 / 127.0),
                         reads=[bs], writes=[b_iscale])
                    P.op("dve", lambda E: E.reciprocal(rin[r][:, :], amax[r][:, :]),
                         reads=[bs], writes=[bs])
                    P.op("dve", lambda E: E.tensor_scalar_mul(rin[r][:, :], rin[r][:, :], 127.0),
                         reads=[bs], writes=[bs])
                    P.op("dve", lambda E: E.tensor_scalar_mul(sq[r][:, :], xnf[r][:, :], rin[r][:, :]),
                         reads=[bs], writes=[bs])
                    P.op("dve", lambda E: E.tensor_copy(qi32[r][:, :], sq[r][:, :]),
                         reads=[bs], writes=[bs])
                    P.op("dve", lambda E: E.tensor_copy(xqb[r][:, :], qi32[r][:, :]),
                         reads=[bs], writes=[b_xqb[r]])
                th.append(qn)
                for half in range(2):
                    def txn(half=half):
                        bt = b_trp[half]
                        bst = b_tst[half]
                        for j in range(8):
                            fc = half * 8 + j
                            P.op("pe", (lambda fc, j: lambda E: E.transpose(
                                trp[half][:, j * 128:(j + 1) * 128], xnb[r][:, fc * 128:(fc + 1) * 128],
                                identb[:, :]))(fc, j),
                                reads=[b_xnb[r], b_consts], writes=[bt])
                        P.op("act", lambda E: E.copy(out=tstage[half][:, :], in_=trp[half][:, :]),
                             reads=[bt], writes=[bst])
                        hf = tc // 4
                        tl = tc % 4
                        P.dma("pool", lambda E: E.dma_start(
                            out=xn_r[hf][:, half * 8:(half + 1) * 8, tl * 128:(tl + 1) * 128],
                            in_=tstage[half][:, :].rearrange("p (f t) -> p f t", t=128)),
                            reads=[bst], writes=[b_xn_in[hf]])
                    th.append(txn)
                for half in range(2):
                    def txq(half=half):
                        bt = b_trp[half]
                        for j in range(8):
                            fc = half * 8 + j
                            P.op("pe", (lambda fc, j: lambda E: E.transpose(
                                trp[half][:, j * 128:(j + 1) * 128], xqb[r][:, fc * 128:(fc + 1) * 128],
                                identb[:, :]))(fc, j),
                                reads=[b_xqb[r], b_consts], writes=[bt])
                        off = (tc * 16 + half * 8) * 128
                        P.op("act", lambda E: E.copy(out=xqf[:, off:off + 1024], in_=trp[half][:, :]),
                             reads=[bt], writes=[b_xqf[tc]])
                    th.append(txq)
                return th

            def ffn_thunks(tc):
                r = tc % 2
                th = []
                for ng in range(4):
                    def mm(ng=ng):
                        for fc in range(NF):
                            off = (tc * 16 + fc) * 128
                            P.op("pe", (lambda fc, ng, off: lambda E: E.matmul(
                                ffn_ps[ng][:, :], xqf[:, off:off + 128],
                                wfb[fc][:, ng * 512:(ng + 1) * 512],
                                start=(fc == 0), stop=(fc == NF - 1)))(fc, ng, off),
                                reads=[b_xqf[tc], b_wfb[fc]], writes=[b_ffnps[ng]])
                    th.append(mm)
                for ng in range(4):
                    def ep(ng=ng):
                        P.op("dve", lambda E: E.tensor_mul(u1[:, :], ffn_ps[ng][:, :],
                                                           wsb[:, ng * 512:(ng + 1) * 512]),
                             reads=[b_ffnps[ng], b_bcast], writes=[b_u])
                        P.op("dve", lambda E: E.tensor_scalar_mul(u1[:, :], u1[:, :], iscale[:, tc:tc + 1]),
                             reads=[b_u, b_iscale], writes=[b_u])
                        P.op("dve", lambda E: E.tensor_add(u2[:, :], u1[:, :],
                                                           biasb[:, ng * 512:(ng + 1) * 512]),
                             reads=[b_u, b_bcast], writes=[b_u])
                        P.op("act", lambda E: E.copy(out=g16[:, :], in_=u2[:, :]),
                             reads=[b_u], writes=[b_u])
                        P.op("act", lambda E: E.activation(out=gst[r][:, ng * 512:(ng + 1) * 512],
                                                           in_=g16[:, :], func=AF.Silu),
                             reads=[b_u], writes=[b_gst[r]])
                    th.append(ep)
                th.append(lambda: P.dma("pool", lambda E: E.dma_start(
                    out=g_sp.ap()[tc * 128:(tc + 1) * 128, :], in_=gst[r][:, :]),
                    reads=[b_gst[r]], writes=[b_g_sp]))
                return th

            EXP_SERIAL_FFN = False
            if EXP_SERIAL_FFN:
                row_prep()
                for t in prep_thunks():
                    t()
                for tc in range(NT):
                    for t in p0_thunks(tc):
                        t()
                for tc in range(NT):
                    for t in ffn_thunks(tc):
                        t()
            else:
                row_prep()
                pending = [prep_thunks(), []]
                for tc in range(NT):
                    merged_emit_generic(p0_thunks(tc), pending.pop(0))
                    pending.append(ffn_thunks(tc))
                    if tc == 3:
                        oa = P.op("pool", lambda E: E.collective_compute(
                            "AllGather", ALU.bypass, replica_groups=[list(range(NCORE))],
                            ins=[xn_in[0].ap().opt()], outs=[xn_gath[0].ap().opt()]),
                            reads=[b_xn_in[0]], writes=[b_xn_gath[0]])
                        oa.is_cc = True
                        P.cc_ops.append(oa)
                for lst in pending:
                    for t in lst:
                        t()
            o = P.op("pool", lambda E: E.collective_compute(
                "AllGather", ALU.bypass, replica_groups=[list(range(NCORE))],
                ins=[xn_in[1].ap().opt()], outs=[xn_gath[1].ap().opt()]),
                reads=[b_xn_in[1]], writes=[b_xn_gath[1]])
            o.is_cc = True
            P.cc_ops.append(o)
            P.barrier()

        p23 = ctx.enter_context(ExitStack())
        q_fm = p23.enter_context(nc.sbuf_tensor("q_fm", [128, HPC * L], BF16))
        k_fm = p23.enter_context(nc.sbuf_tensor("k_fm", [128, HPC * L], BF16))
        v_tm = p23.enter_context(nc.sbuf_tensor("v_tm", [128, HPC * L], BF16))

        with ExitStack() as p2:
            mt = [p2.enter_context(nc.sbuf_tensor(f"mt{i}", [128, 128], BF16)) for i in range(6)]
            mtf = p2.enter_context(nc.sbuf_tensor("mtf", [128, 128], F32))
            wraw = [p2.enter_context(nc.sbuf_tensor(f"wraw{i}", [128, CDIM], F32)) for i in range(2)]
            wrb = p2.enter_context(nc.sbuf_tensor("wrb", [128, CDIM], BF16))
            wTr = p2.enter_context(nc.sbuf_tensor("wTr", [128, 96 * 128], BF16))
            xng = [p2.enter_context(nc.sbuf_tensor(f"xng{i}", [128, NF * 512], BF16)) for i in range(2)]
            vstage = p2.enter_context(nc.sbuf_tensor("vstage", [128, 512], BF16))
            rope_ps = p2.enter_context(nc.psum_tensor("rope_ps", [128, 512], F32))
            vt_ps = p2.enter_context(nc.psum_tensor("vt_ps", [128, 1024], BF16))
            qkv_ps = [p2.enter_context(nc.psum_tensor(f"qkv_ps{i}", [128, 512], F32)) for i in range(6)]

            b_mt = Buf("mt")
            b_wraw = [Buf("wraw0"), Buf("wraw1")]
            b_wrb = Buf("wrb")
            b_wTr = Buf("wTr")
            b_ropeps = Buf("rope_ps")
            b_xng = [Buf("xng0"), Buf("xng1")]
            b_qkvps = [Buf(f"qkv_ps{i}") for i in range(6)]
            b_vstage = Buf("vstage")
            b_vtps = Buf("vt_ps")

            for jc in range(6):
                P.dma("sp", (lambda jc: lambda E: E.dma_start(
                    out=mtf[:, :], in_=mrot.ap()[jc * 128:(jc + 1) * 128, :]))(jc),
                    writes=[b_mt])
                P.op("dve", (lambda jc: lambda E: E.tensor_copy(mt[jc][:, :], mtf[:, :]))(jc),
                     reads=[b_mt], writes=[b_mt])
            for jc in range(6):
                r = jc % 2
                P.dma("sp", (lambda jc, r: lambda E: E.dma_start(
                    out=wraw[r][:, :], in_=w_qkv.ap()[jc * 128:(jc + 1) * 128, :]))(jc, r),
                    writes=[b_wraw[r]])
                P.op("dve", (lambda r: lambda E: E.tensor_copy(wrb[:, :], wraw[r][:, :]))(r),
                     reads=[b_wraw[r]], writes=[b_wrb])
                for fq in range(4):
                    for j in range(4):
                        fc = fq * 4 + j
                        P.op("pe", (lambda jc, fc, j: lambda E: E.matmul(
                            rope_ps[:, j * 128:(j + 1) * 128], wrb[:, fc * 128:(fc + 1) * 128],
                            mt[jc][:, :], start=True, stop=True))(jc, fc, j),
                            reads=[b_wrb, b_mt], writes=[b_ropeps])
                    for j in range(4):
                        fc = fq * 4 + j
                        off = (fc * 6 + jc) * 128
                        P.op("act", (lambda off, j: lambda E: E.copy(
                            out=wTr[:, off:off + 128], in_=rope_ps[:, j * 128:(j + 1) * 128]))(off, j),
                            reads=[b_ropeps], writes=[b_wTr])
            t16_order = [t for t in range(NS) if t % 2 == 0] + [t for t in range(NS) if t % 2 == 1]
            for idx, t16 in enumerate(t16_order):
                r = idx % 2
                rank, half = t16 // 2, t16 % 2
                for fc in range(NF):
                    P.dma("sp", (lambda rank, half, fc, r: lambda E: E.dma_start(
                        out=xng[r][:, fc * 512:(fc + 1) * 512],
                        in_=xn_gath[half].ap()[rank * CDIM + fc * 128: rank * CDIM + (fc + 1) * 128, :]))(rank, half, fc, r),
                        reads=[b_xn_gath[half]], writes=[b_xng[r]])
                for jc in range(6):
                    for fc in range(NF):
                        off = (fc * 6 + jc) * 128
                        P.op("pe", (lambda jc, fc, off, r: lambda E: E.matmul(
                            qkv_ps[jc][:, :], wTr[:, off:off + 128], xng[r][:, fc * 512:(fc + 1) * 512],
                            start=(fc == 0), stop=(fc == NF - 1)))(jc, fc, off, r),
                            reads=[b_wTr, b_xng[r]], writes=[b_qkvps[jc]])
                for jc in range(4):
                    dst = q_fm if jc < 2 else k_fm
                    bdst = b_qfm if jc < 2 else b_kfm
                    h = jc % 2
                    P.op("act", (lambda jc, h, t16, dst: lambda E: E.copy(
                        out=dst[:, h * L + t16 * 512: h * L + (t16 + 1) * 512], in_=qkv_ps[jc][:, :]))(jc, h, t16, dst),
                        reads=[b_qkvps[jc]], writes=[bdst])
                for jc in range(4, 6):
                    h = jc - 4
                    P.op("act", (lambda jc: lambda E: E.copy(out=vstage[:, :], in_=qkv_ps[jc][:, :]))(jc),
                         reads=[b_qkvps[jc]], writes=[b_vstage])
                    for a in range(4):
                        P.op("pe", (lambda a: lambda E: E.transpose(
                            vt_ps[:, a * 128:(a + 1) * 128], vstage[:, a * 128:(a + 1) * 128],
                            identb[:, :]))(a),
                            reads=[b_vstage, b_consts], writes=[b_vtps])
                    P.op("act", (lambda h, t16: lambda E: E.copy(
                        out=v_tm[:, h * L + t16 * 512: h * L + (t16 + 1) * 512], in_=vt_ps[:, 0:512]))(h, t16),
                        reads=[b_vtps], writes=[b_vtm])
            if DEBUG:
                P.dma("pool", lambda E: E.dma_start(out=dbg_q.ap(), in_=q_fm[:, :]), reads=[b_qfm])
                P.dma("pool", lambda E: E.dma_start(out=dbg_k.ap(), in_=k_fm[:, :]), reads=[b_kfm])
                P.dma("pool", lambda E: E.dma_start(out=dbg_v.ap(), in_=v_tm[:, :]), reads=[b_vtm])
            P.barrier()

        wpb = [ctx.enter_context(nc.sbuf_tensor(f"wpb{i}", [128, CDIM], BF16, side="right")) for i in range(NF)]
        wpr = [ctx.enter_context(nc.sbuf_tensor(f"wpr{i}", [128, CDIM], F32, side="right")) for i in range(2)]
        b_wpb = [Buf(f"wpb{i}") for i in range(NF)]
        b_wpr = [Buf("wpr0"), Buf("wpr1")]
        for cc in range(NF):
            r = cc % 2
            P.dma("sp", (lambda cc, r: lambda E: E.dma_start(
                out=wpr[r][:, :], in_=wprojT.ap()[cc * 128:(cc + 1) * 128, :]))(cc, r),
                writes=[b_wpr[r]])
            P.op("dve", (lambda cc, r: lambda E: E.tensor_copy(wpb[cc][:, :], wpr[r][:, :]))(cc, r),
                 reads=[b_wpr[r]], writes=[b_wpb[cc]])

        with ExitStack() as p3:
            m_col = [p3.enter_context(nc.sbuf_tensor(f"m_col{i}", [128, 4], F32)) for i in range(2)]
            mneg = [p3.enter_context(nc.sbuf_tensor(f"mneg{i}", [1, 512], BF16)) for i in range(2)]
            mbneg = [p3.enter_context(nc.sbuf_tensor(f"mbneg{i}", [128, 512], BF16)) for i in range(2)]
            pT = [p3.enter_context(nc.sbuf_tensor(f"pT{i}", [128, 512], BF16)) for i in range(4)]
            bm = [p3.enter_context(nc.sbuf_tensor(f"bm{i}", [128, 1], F32)) for i in range(2)]
            l_sb = p3.enter_context(nc.sbuf_tensor("l_sb", [1, 512], F32))
            lr_sb = p3.enter_context(nc.sbuf_tensor("lr_sb", [1, 512], F32))
            bcf = p3.enter_context(nc.sbuf_tensor("bcf", [128, 512], F32))
            ofin = [p3.enter_context(nc.sbuf_tensor(f"ofin{i}", [128, 512], BF16)) for i in range(2)]
            s1_ps = [p3.enter_context(nc.psum_tensor("s1_ps0", [128, 512], F32))]
            sT_ps = [p3.enter_context(nc.psum_tensor(f"sT_ps{i}", [128, 512], F32)) for i in range(3)]
            o_ps = p3.enter_context(nc.psum_tensor("o_ps", [128, 512], F32))
            l_ps = p3.enter_context(nc.psum_tensor("l_ps", [1, 512], F32))
            mtr_ps = p3.enter_context(nc.psum_tensor("mtr_ps", [1, 512], F32))
            bc2_ps = p3.enter_context(nc.psum_tensor("bc2_ps", [128, 512], F32))

            b_mcol = [Buf("m_col0"), Buf("m_col1")]
            b_mneg = [Buf("mneg0"), Buf("mneg1")]
            b_mbneg = [Buf("mbneg0"), Buf("mbneg1")]
            b_pT = [Buf(f"pT{i}") for i in range(4)]
            b_bm = [Buf("bm0"), Buf("bm1")]
            b_lsb = Buf("l_sb"); b_bcf = Buf("bcf")
            b_ofin = [Buf("ofin0"), Buf("ofin1")]
            b_s1 = [Buf("s1_ps0")]
            b_sT = [Buf(f"sT_ps{i}") for i in range(3)]
            b_ops = Buf("o_ps"); b_lps = Buf("l_ps")
            b_mtr = Buf("mtr_ps"); b_bc2 = Buf("bc2_ps")

            def pass1_thunks(s, h, pp):
                hb = h * L
                th = []
                th.append(lambda: P.op("dve", lambda E: E.memset(m_col[pp][:, :], NEG_BIG),
                                       writes=[b_mcol[pp]]))
                items = [(kb, qi) for kb in range(s + 1) for qi in range(4)]
                for idx, (kb, qi) in enumerate(items):
                    rot = 0
                    width = (qi + 1) * 128 if kb == s else 512

                    def stats_tile(kb=kb, qi=qi, rot=rot, width=width, hb=hb, s=s, pp=pp):
                        bs = b_s1[rot]
                        P.op("pe", lambda E: E.matmul(
                            s1_ps[rot][:, 0:width],
                            q_fm[:, hb + s * 512 + qi * 128: hb + s * 512 + (qi + 1) * 128],
                            k_fm[:, hb + kb * 512: hb + kb * 512 + width],
                            start=True, stop=True),
                            reads=[b_qfm, b_kfm], writes=[bs])
                        if kb == s:
                            P.op("dve", lambda E: E.tensor_add(
                                s1_ps[rot][:, qi * 128:(qi + 1) * 128],
                                s1_ps[rot][:, qi * 128:(qi + 1) * 128], maskA[:, :]),
                                reads=[bs, b_consts], writes=[bs])
                        P.op("dve", lambda E: E.reduce_max(
                            bm[pp][:, :], s1_ps[rot][:, 0:width], axis=AXX),
                            reads=[bs], writes=[b_bm[pp]])
                        P.op("dve", lambda E: E.tensor_max(
                            m_col[pp][:, qi:qi + 1], m_col[pp][:, qi:qi + 1], bm[pp][:, :]),
                            reads=[b_bm[pp], b_mcol[pp]], writes=[b_mcol[pp]])
                    th.append(stats_tile)

                def fin(pp=pp):
                    P.op("dve", lambda E: E.tensor_scalar_max(m_col[pp][:, :], m_col[pp][:, :], M_CLAMP),
                         reads=[b_mcol[pp]], writes=[b_mcol[pp]])
                    for qi in range(4):
                        P.op("pe", (lambda qi: lambda E: E.transpose(
                            mtr_ps[:, qi * 128:(qi + 1) * 128], m_col[pp][:, qi:qi + 1], identf[:, :]))(qi),
                            reads=[b_mcol[pp], b_consts], writes=[b_mtr])
                    P.op("act", lambda E: E.activation(out=mneg[pp][:, :], in_=mtr_ps[0:1, :], func=AF.Copy,
                                                       bias=0.0, scale=-1.0),
                         reads=[b_mtr], writes=[b_mneg[pp]])
                    P.op("pe", lambda E: E.matmul(bc2_ps[:, :], onesrb[:, :], mneg[pp][:, :],
                                                  start=True, stop=True),
                         reads=[b_mneg[pp], b_consts], writes=[b_bc2])
                    P.op("act", lambda E: E.copy(out=mbneg[pp][:, :], in_=bc2_ps[:, :]),
                         reads=[b_bc2], writes=[b_mbneg[pp]])
                th.append(fin)
                return th

            def pass2_thunks(s, h, pp):
                hb = h * L
                n_iter = (s + 1) * 4
                th = []
                items = [(kb, a) for kb in range(s + 1) for a in range(4)]

                def front(kb, a, rot, use_pe_shift):
                    bs = b_sT[rot]
                    bp = b_pT[a]
                    q0 = a * 128 if kb == s else 0
                    if use_pe_shift:
                        P.op("pe", lambda E: E.matmul(
                            sT_ps[rot][:, :], onesrb[:, :], mneg[pp][:, :], start=True, stop=False),
                            reads=[b_mneg[pp], b_consts], writes=[bs])
                        P.op("pe", lambda E: E.matmul(
                            sT_ps[rot][:, q0:512],
                            k_fm[:, hb + kb * 512 + a * 128: hb + kb * 512 + (a + 1) * 128],
                            q_fm[:, hb + s * 512 + q0: hb + (s + 1) * 512],
                            start=False, stop=True),
                            reads=[b_kfm, b_qfm], writes=[bs])
                    else:
                        P.op("pe", lambda E: E.matmul(
                            sT_ps[rot][:, q0:512],
                            k_fm[:, hb + kb * 512 + a * 128: hb + kb * 512 + (a + 1) * 128],
                            q_fm[:, hb + s * 512 + q0: hb + (s + 1) * 512],
                            start=True, stop=True),
                            reads=[b_kfm, b_qfm], writes=[bs])
                        if q0 > 0:
                            P.op("dve", lambda E: E.memset(sT_ps[rot][:, 0:q0], 0.0),
                                 reads=[bs], writes=[bs])
                        P.op("dve", lambda E: E.tensor_add(
                            sT_ps[rot][:, :], sT_ps[rot][:, :], mbneg[pp][:, :]),
                            reads=[bs, b_mbneg[pp]], writes=[bs])
                    if kb == s:
                        P.op("dve", lambda E: E.tensor_add(
                            sT_ps[rot][:, a * 128:(a + 1) * 128],
                            sT_ps[rot][:, a * 128:(a + 1) * 128], maskB[:, :]),
                            reads=[bs, b_consts], writes=[bs])
                    P.op("act", lambda E: E.activation(
                        out=pT[a][:, :], in_=sT_ps[rot][:, :], func=AF.Exp),
                        reads=[bs], writes=[bp])
                    if kb == s and a > 0:
                        P.op("dve", lambda E: E.memset(pT[a][:, 0:a * 128], 0.0),
                             reads=[], writes=[bp])

                def back(kb, a, first, last):
                    bp = b_pT[a]
                    P.op("pe", lambda E: E.matmul(
                        o_ps[:, :], v_tm[:, hb + (kb * 4 + a) * 128: hb + (kb * 4 + a + 1) * 128],
                        pT[a][:, :], start=first, stop=last),
                        reads=[b_vtm, bp], writes=[b_ops])
                    P.op("pe", lambda E: E.matmul(
                        l_ps[:, :], onescb[:, :], pT[a][:, :], start=first, stop=last),
                        reads=[b_consts, bp], writes=[b_lps])

                for it, (kb, a) in enumerate(items):
                    def step(it=it, kb=kb, a=a):
                        front(kb, a, it % 3, it % 2 == 0)
                        if it > 1:
                            pk, pa = items[it - 2]
                            back(pk, pa, it - 2 == 0, False)
                    th.append(step)

                def fin(s=s, h=h):
                    for it in range(max(0, n_iter - 2), n_iter):
                        pk, pa = items[it]
                        back(pk, pa, it == 0, it == n_iter - 1)
                    P.op("dve", lambda E: E.tensor_scalar_add(l_sb[:, :], l_ps[0:1, :], 1e-30),
                         reads=[b_lps], writes=[b_lsb])
                    P.op("act", lambda E: E.activation(out=l_sb[:, :], in_=l_sb[:, :], func=AF.Ln),
                         reads=[b_lsb], writes=[b_lsb])
                    P.op("act", lambda E: E.activation(out=lr_sb[:, :], in_=l_sb[:, :], func=AF.Exp,
                                                       bias=0.0, scale=-1.0),
                         reads=[b_lsb], writes=[b_lsb])
                    P.op("pe", lambda E: E.matmul(bc2_ps[:, :], onesr[:, :], lr_sb[:, :], start=True, stop=True),
                         reads=[b_lsb, b_consts], writes=[b_bc2])
                    P.op("act", lambda E: E.copy(out=bcf[:, :], in_=bc2_ps[:, :]),
                         reads=[b_bc2], writes=[b_bcf])
                    fr = (s * 2 + h) % 2
                    P.op("dve", lambda E: E.tensor_mul(ofin[fr][:, :], o_ps[:, :], bcf[:, :]),
                         reads=[b_ops, b_bcf], writes=[b_ofin[fr]])
                    P.dma("pool", lambda E: E.dma_start(
                        out=a2a_in.ap()[(s // 2) * 256 + h * 128: (s // 2) * 256 + (h + 1) * 128,
                                        (s % 2) * 512:((s % 2) + 1) * 512],
                        in_=ofin[fr][:, :]),
                        reads=[b_ofin[fr]], writes=[b_a2a_in])
                th.append(fin)
                return th

            def merged_emit(l1, l2):
                n1, n2 = len(l1), len(l2)
                i1 = i2 = 0
                while i1 < n1 or i2 < n2:
                    if i2 < n2 and (i1 >= n1 or i1 * n2 >= i2 * n1):
                        l2[i2](); i2 += 1
                    else:
                        l1[i1](); i1 += 1

            iters = [(s, h) for s in range(NS) for h in range(HPC)]
            for t in pass1_thunks(*iters[0], 0):
                t()
            for i in range(len(iters)):
                nxt = pass1_thunks(*iters[i + 1], (i + 1) % 2) if i + 1 < len(iters) else []
                merged_emit(nxt, pass2_thunks(*iters[i], i % 2))

            def cc2(E):
                with allow_ici_all_to_all():
                    return E.collective_compute(
                        "AllToAll", ALU.bypass, replica_groups=[list(range(NCORE))],
                        ins=[a2a_in.ap().opt()], outs=[a2a_out.ap().opt()])
            o = P.op("pool", cc2, reads=[b_a2a_in], writes=[b_a2a_out])
            o.is_cc = True
            P.cc_ops.append(o)
            P.barrier()
        p23.close()

        with ExitStack() as p5:
            oat = [p5.enter_context(nc.sbuf_tensor(f"oat{i}", [128, TOWN], BF16)) for i in range(NF)]
            xt2 = [p5.enter_context(nc.sbuf_tensor(f"xt2_{i}", [128, CDIM], F32)) for i in range(2)]
            gt = [p5.enter_context(nc.sbuf_tensor(f"gt{i}", [128, CDIM], FP16)) for i in range(2)]
            gt32 = p5.enter_context(nc.sbuf_tensor("gt32", [128, 512], F32))
            r1 = p5.enter_context(nc.sbuf_tensor("r1", [128, 512], F32))
            outst = [p5.enter_context(nc.sbuf_tensor(f"outst{i}", [128, CDIM], F32)) for i in range(2)]
            op_ps = [p5.enter_context(nc.psum_tensor(f"op_ps{i}", [128, 512], F32)) for i in range(8)]

            b_oat = [Buf(f"oat{i}") for i in range(NF)]
            b_xt2 = [Buf("xt2_0"), Buf("xt2_1")]
            b_gt = [Buf("gt0"), Buf("gt1")]
            b_r = Buf("p5_scratch")
            b_opps = [Buf(f"op_ps{i}") for i in range(8)]
            b_outst = [Buf("outst0"), Buf("outst1")]

            for cc in range(NF):
                P.dma("sp", (lambda cc: lambda E: E.dma_start(
                    out=oat[cc][:, :], in_=a2a_out.ap()[cc * 128:(cc + 1) * 128, :]))(cc),
                    reads=[b_a2a_out], writes=[b_oat[cc]])
            for tc in range(NT):
                r = tc % 2
                P.dma("sp", (lambda tc, r: lambda E: E.dma_start(
                    out=xt2[r][:, :], in_=x_own.ap()[tc * 128:(tc + 1) * 128, :]))(tc, r),
                    writes=[b_xt2[r]])
                P.dma("sp", (lambda tc, r: lambda E: E.dma_start(
                    out=gt[r][:, :], in_=g_sp.ap()[tc * 128:(tc + 1) * 128, :]))(tc, r),
                    reads=[b_g_sp], writes=[b_gt[r]])
                for ng in range(4):
                    pi = (tc % 2) * 4 + ng
                    for cc in range(NF):
                        P.op("pe", (lambda tc, ng, cc, pi: lambda E: E.matmul(
                            op_ps[pi][:, :], oat[cc][:, tc * 128:(tc + 1) * 128],
                            wpb[cc][:, ng * 512:(ng + 1) * 512],
                            start=(cc == 0), stop=(cc == NF - 1)))(tc, ng, cc, pi),
                            reads=[b_oat[cc], b_wpb[cc]], writes=[b_opps[pi]])
                for ng in range(4):
                    pi = (tc % 2) * 4 + ng
                    P.op("act", (lambda r, ng: lambda E: E.copy(
                        out=gt32[:, :], in_=gt[r][:, ng * 512:(ng + 1) * 512]))(r, ng),
                        reads=[b_gt[r]], writes=[b_r])
                    P.op("dve", (lambda pi: lambda E: E.tensor_mul(r1[:, :], op_ps[pi][:, :], gt32[:, :]))(pi),
                         reads=[b_opps[pi], b_r], writes=[b_r])
                    P.op("dve", (lambda r, ng: lambda E: E.tensor_add(
                        outst[r][:, ng * 512:(ng + 1) * 512], r1[:, :],
                        xt2[r][:, ng * 512:(ng + 1) * 512]))(r, ng),
                         reads=[b_r, b_xt2[r]], writes=[b_outst[r]])
                P.dma("pool", (lambda tc, r: lambda E: E.dma_start(
                    out=out_c.ap()[tc * 128:(tc + 1) * 128, :], in_=outst[r][:, :]))(tc, r),
                    reads=[b_outst[r]], writes=[b_out])
            if DEBUG:
                P.dma("pool", lambda E: E.dma_start(out=dbg_g.ap(), in_=g_sp.ap()),
                      reads=[b_g_sp])
                P.dma("pool", lambda E: E.dma_start(out=dbg_ai.ap(), in_=a2a_in.ap()),
                      reads=[b_a2a_in])
                P.dma("pool", lambda E: E.dma_start(out=dbg_ao.ap(), in_=a2a_out.ap()),
                      reads=[b_a2a_out])
            P.barrier()

        sems, runner = P.materialize(nc, ctx)
        block = ctx.enter_context(nc.Block())
        block.tensor(runner("pe"))
        block.scalar(runner("act"))
        block.vector(runner("dve"))
        block.gpsimd(runner("pool"))
        block.sync(runner("sp"))

    return nc


_NC_CACHE = None


def kernel(L0_attn_x, L0_attn_w_attn, L0_attn_w_proj, L0_ffn_weight, L0_ffn_w_scale, L0_ffn_bias):
    global _NC_CACHE
    x = np.asarray(L0_attn_x)
    w_attn = np.asarray(L0_attn_w_attn)
    w_proj = np.asarray(L0_attn_w_proj)
    w_ffn = np.asarray(L0_ffn_weight)
    w_scale = np.asarray(L0_ffn_w_scale)
    bias = np.asarray(L0_ffn_bias)

    wprojT_np = np.ascontiguousarray(w_proj.T.astype(np.float32))
    wffnT_np = np.ascontiguousarray(w_ffn.T)
    wscale_np = np.ascontiguousarray(w_scale.reshape(1, CDIM).astype(np.float32))
    bias_np = np.ascontiguousarray(bias.reshape(1, CDIM).astype(np.float32))

    if _NC_CACHE is None:
        _NC_CACHE = build_nc()
    nc = _NC_CACHE

    in_maps = []
    for c in range(NCORE):
        rows = slice(c * 256, (c + 1) * 256)
        w_qkv_own = np.ascontiguousarray(np.concatenate(
            [w_attn[rows], w_attn[CDIM + rows.start: CDIM + rows.stop],
             w_attn[2 * CDIM + rows.start: 2 * CDIM + rows.stop]], axis=0).astype(np.float32))
        in_maps.append({
            "x_own": np.ascontiguousarray(x[0, c * TOWN:(c + 1) * TOWN, :].astype(np.float32)),
            "w_qkv": w_qkv_own,
            "mrot": _m_blocks(c),
            "wprojT": wprojT_np,
            "wffnT": wffnT_np,
            "wscale_r": wscale_np,
            "bias_r": bias_np,
        })
    res = run_bass_kernel_spmd(nc, in_maps, core_ids=list(range(NCORE)))
    globals()["LAST_RES"] = res
    out = np.empty((1, L, CDIM), dtype=np.float32)
    for c in range(NCORE):
        out[0, c * TOWN:(c + 1) * TOWN, :] = res.results[c]["out_c"]
    return out


# version 19
# speedup vs baseline: 1.0648x; 1.0648x over previous
import numpy as np
from contextlib import ExitStack

import concourse.bass as bass
import concourse.mybir as mybir
from concourse.bass_utils import run_bass_kernel_spmd
from concourse.aot_env import allow_ici_all_to_all
import ml_dtypes

F32 = mybir.dt.float32
BF16 = mybir.dt.bfloat16
FP16 = mybir.dt.float16
I8 = mybir.dt.int8
I32 = mybir.dt.int32
AF = mybir.ActivationFunctionType
ALU = mybir.AluOpType
AXX = mybir.AxisListType.X
NPBF = ml_dtypes.bfloat16

L = 8192
CDIM = 2048
H = 16
D = 128
NCORE = 8
HPC = H // NCORE
TOWN = L // NCORE
NT = TOWN // 128
NF = CDIM // 128
NS = L // 512
NEG_BIG = -1.0e30
M_CLAMP = -80.0

ENGS = ("pe", "act", "dve", "pool", "sp")
DEBUG = False


class Buf:
    __slots__ = ("name", "last_write", "readers")

    def __init__(self, name):
        self.name = name
        self.last_write = None
        self.readers = []


class Op:
    __slots__ = ("eng", "build", "needs_inc", "waits", "is_dma", "inc_amt", "val",
                 "sem_key", "is_cc", "dma_buf")

    def __init__(self, eng, build, is_dma):
        self.eng = eng
        self.build = build
        self.needs_inc = is_dma
        self.waits = []
        self.is_dma = is_dma
        self.inc_amt = 16 if is_dma else 1
        self.val = None
        self.sem_key = None
        self.is_cc = False
        self.dma_buf = None


class Plan:

    def __init__(self):
        self.ops = {e: [] for e in ENGS}
        self.cc_ops = []

    def _dep(self, op, dep_op):
        if dep_op is None:
            return
        if dep_op is op:
            return
        if (dep_op.eng == op.eng == "pe") and (not dep_op.is_dma) and (not op.is_dma):
            return
        dep_op.needs_inc = True
        op.waits.append(dep_op)

    def op(self, eng, build, reads=(), writes=(), is_dma=False):
        o = Op(eng, build, is_dma)
        for b in reads:
            self._dep(o, b.last_write)
        for b in writes:
            self._dep(o, b.last_write)
            for r in b.readers:
                self._dep(o, r)
        self.ops[eng].append(o)
        for b in writes:
            b.last_write = o
            b.readers = []
        for b in reads:
            b.readers.append(o)
        return o

    def dma(self, queue, build, reads=(), writes=()):
        o = self.op(queue, build, reads=reads, writes=writes, is_dma=True)
        o.dma_buf = writes[0] if writes else None
        return o

    def barrier(self):
        tails = {}
        for e in ENGS:
            for o in reversed(self.ops[e]):
                if not o.is_cc and o.build is not None:
                    tails[e] = o
                    break
        for e in ENGS:
            o = Op(e, None, False)
            for e2, t in tails.items():
                if e2 != e:
                    t.needs_inc = True
                    o.waits.append(t)
            self.ops[e].append(o)

    def cc_barrier(self, ccop):
        ccop.needs_inc = True
        for e in ENGS:
            o = Op(e, None, False)
            o.waits.append(ccop)
            self.ops[e].append(o)

    def materialize(self, nc, ctx):
        sems = {}
        for e in ENGS:
            sems[e] = ctx.enter_context(nc.semaphore("s_" + e))
            sems[e + "_dma"] = ctx.enter_context(nc.semaphore("s_" + e + "_dma"))
        sems["cc"] = ctx.enter_context(nc.semaphore("s_cc"))
        buf_sem_cnt = {}
        for e in ENGS:
            for o in self.ops[e]:
                if o.is_dma and o.dma_buf is not None:
                    key = "db_" + o.dma_buf.name
                    if key not in sems:
                        sems[key] = ctx.enter_context(nc.semaphore(key))
                        buf_sem_cnt[key] = 0
        for e in ENGS:
            c_cnt = 0
            d_cnt = 0
            for o in self.ops[e]:
                if getattr(o, "is_cc", False):
                    continue
                if o.needs_inc:
                    if o.is_dma:
                        if o.dma_buf is not None:
                            key = "db_" + o.dma_buf.name
                            buf_sem_cnt[key] += 16
                            o.val = buf_sem_cnt[key]
                            o.sem_key = key
                        else:
                            d_cnt += 16
                            o.val = d_cnt
                            o.sem_key = e + "_dma"
                    else:
                        c_cnt += 1
                        o.val = c_cnt
                        o.sem_key = e
        cc_cnt = 0
        for o in self.cc_ops:
            cc_cnt += 1
            o.val = cc_cnt
            o.sem_key = "cc"

        def runner(eng_name):
            def run(E):
                seen = {}
                for o in self.ops[eng_name]:
                    want = {}
                    for dep in o.waits:
                        k = dep.sem_key
                        v = dep.val
                        if v is None:
                            raise RuntimeError("dep without milestone")
                        if want.get(k, -1) < v:
                            want[k] = v
                    for k, v in sorted(want.items()):
                        if seen.get(k, -1) >= v:
                            continue
                        E.wait_ge(sems[k], v)
                        seen[k] = v
                    if o.build is None:
                        continue
                    instr = o.build(E)
                    if o.needs_inc:
                        instr.then_inc(sems[o.sem_key], o.inc_amt)
            return run
        return sems, runner


def _rope_tables():
    inv_freq = (1.0 / (10000.0 ** (np.arange(0, D, 2, dtype=np.float32) / np.float32(D)))).astype(np.float32)
    theta = (np.arange(H, dtype=np.float32)[:, None] * inv_freq[None, :]).astype(np.float32)
    return np.cos(theta).astype(np.float32), np.sin(theta).astype(np.float32)


def _m_blocks(core):
    cos, sin = _rope_tables()
    scale = np.float32(1.0) / np.sqrt(np.float32(D)).astype(np.float32)
    out = np.zeros((6, D, D), dtype=np.float32)
    for j, (kind, h) in enumerate([("q", 2 * core), ("q", 2 * core + 1),
                                   ("k", 2 * core), ("k", 2 * core + 1),
                                   ("v", 0), ("v", 0)]):
        if kind == "v":
            out[j] = np.eye(D, dtype=np.float32)
            continue
        m = np.zeros((D, D), dtype=np.float32)
        for i in range(D // 2):
            c = cos[h, i]; s = sin[h, i]
            m[2 * i, 2 * i] = c
            m[2 * i + 1, 2 * i] = -s
            m[2 * i, 2 * i + 1] = s
            m[2 * i + 1, 2 * i + 1] = c
        if kind == "q":
            m = (m * scale).astype(np.float32)
        out[j] = m
    return out.reshape(6 * D, D)


def build_nc():
    nc = bass.Bass("TRN2", target_bir_lowering=False)

    x_own = nc.dram_tensor("x_own", [TOWN, CDIM], F32, kind="ExternalInput")
    w_qkv = nc.dram_tensor("w_qkv", [6 * D, CDIM], F32, kind="ExternalInput")
    mrot = nc.dram_tensor("mrot", [6 * D, D], F32, kind="ExternalInput")
    wprojT = nc.dram_tensor("wprojT", [CDIM, CDIM], F32, kind="ExternalInput")
    wffnT = nc.dram_tensor("wffnT", [CDIM, CDIM], I8, kind="ExternalInput")
    wscale_r = nc.dram_tensor("wscale_r", [1, CDIM], F32, kind="ExternalInput")
    bias_r = nc.dram_tensor("bias_r", [1, CDIM], F32, kind="ExternalInput")
    out_c = nc.dram_tensor("out_c", [TOWN, CDIM], F32, kind="ExternalOutput")

    maskA_np = np.where(np.arange(D)[None, :] <= np.arange(D)[:, None], 0.0, NEG_BIG).astype(np.float32)
    maskB_np = maskA_np.T.copy()
    maskA_t = nc.inline_tensor(maskA_np, "maskA_c")
    maskB_t = nc.inline_tensor(maskB_np, "maskB_c")
    identf_t = nc.inline_tensor(np.eye(D, dtype=np.float32), "identf_c")
    identb_t = nc.inline_tensor(np.eye(D, dtype=np.float32).astype(NPBF), "identb_c")

    if DEBUG:
        dbg_xng = nc.dram_tensor("dbg_xng", [NCORE * CDIM, TOWN], BF16, kind="ExternalOutput")
        dbg_q = nc.dram_tensor("dbg_q", [128, HPC * L], BF16, kind="ExternalOutput")
        dbg_k = nc.dram_tensor("dbg_k", [128, HPC * L], BF16, kind="ExternalOutput")
        dbg_v = nc.dram_tensor("dbg_v", [128, HPC * L], BF16, kind="ExternalOutput")
        dbg_g = nc.dram_tensor("dbg_g", [TOWN, CDIM], FP16, kind="ExternalOutput")
        dbg_ai = nc.dram_tensor("dbg_ai", [CDIM, TOWN], BF16, kind="ExternalOutput")
        dbg_ao = nc.dram_tensor("dbg_ao", [CDIM, TOWN], BF16, kind="ExternalOutput")
        dbg_xnin = nc.dram_tensor("dbg_xnin", [CDIM, TOWN], BF16, kind="ExternalOutput")
        dbg_acc = nc.dram_tensor("dbg_acc", [128, CDIM], F32, kind="ExternalOutput")
        dbg_u2 = nc.dram_tensor("dbg_u2", [128, CDIM], F32, kind="ExternalOutput")
        dbg_wsb = nc.dram_tensor("dbg_wsb", [128, CDIM], F32, kind="ExternalOutput")
        dbg_biasb = nc.dram_tensor("dbg_biasb", [128, CDIM], F32, kind="ExternalOutput")
        dbg_wfb = nc.dram_tensor("dbg_wfb", [128, CDIM], BF16, kind="ExternalOutput")
        dbg_ln = nc.dram_tensor("dbg_ln", [128, 8], F32, kind="ExternalOutput")
        dbg_xnf = nc.dram_tensor("dbg_xnf", [128, CDIM], F32, kind="ExternalOutput")

    cdum_in = nc.dram_tensor("cdum_in", [128, 128], F32)
    cdum_out = nc.dram_tensor("cdum_out", [128, 128], F32)

    xn_in = [nc.dram_tensor(f"xn_in{i}", [CDIM, TOWN // 2], BF16) for i in range(2)]
    xn_gath = [nc.dram_tensor(f"xn_gath{i}", [NCORE * CDIM, TOWN // 2], BF16) for i in range(2)]
    g_sp = nc.dram_tensor("g_sp", [TOWN, CDIM], FP16)
    a2a_in = nc.dram_tensor("a2a_in", [CDIM, TOWN], BF16)
    a2a_out = nc.dram_tensor("a2a_out", [CDIM, TOWN], BF16)

    P = Plan()

    with ExitStack() as ctx:
        en = ctx.enter_context

        iscale = en(nc.sbuf_tensor("iscale", [128, NT], F32))
        maskA = en(nc.sbuf_tensor("maskA", [128, 128], F32))
        maskB = en(nc.sbuf_tensor("maskB", [128, 128], F32))
        identf = en(nc.sbuf_tensor("identf", [128, 128], F32))
        identb = en(nc.sbuf_tensor("identb", [128, 128], BF16))
        onesr = en(nc.sbuf_tensor("onesr", [1, 128], F32))
        onesrb = en(nc.sbuf_tensor("onesrb", [1, 128], BF16))
        onescb = en(nc.sbuf_tensor("onescb", [128, 1], BF16))

        b_qfm = Buf("q_fm"); b_kfm = Buf("k_fm"); b_vtm = Buf("v_tm")
        b_iscale = Buf("iscale")
        b_consts = Buf("consts")
        b_xn_in = [Buf("xn_in0"), Buf("xn_in1")]
        b_xn_gath = [Buf("xn_gath0"), Buf("xn_gath1")]
        b_g_sp = Buf("g_sp")
        b_a2a_in = Buf("a2a_in"); b_a2a_out = Buf("a2a_out")
        b_out = Buf("out_c")

        for dst, src in ((maskA, maskA_t), (maskB, maskB_t), (identf, identf_t), (identb, identb_t)):
            P.dma("sp", (lambda d, s: lambda E: E.dma_start(out=d[:, :], in_=s.ap()))(dst, src),
                  writes=[b_consts])
        P.op("dve", lambda E: E.memset(onesr[:, :], 1.0), writes=[b_consts])
        P.op("dve", lambda E: E.memset(onesrb[:, :], 1.0), writes=[b_consts])
        P.op("dve", lambda E: E.memset(onescb[:, :], 1.0), writes=[b_consts])

        b_cdum = Buf("cdum")
        P.dma("sp", lambda E: E.dma_start(out=cdum_in.ap(), in_=identf_t.ap()),
              writes=[b_cdum])

        def merged_emit_generic(l1, l2):
            n1, n2 = len(l1), len(l2)
            i1 = i2 = 0
            while i1 < n1 or i2 < n2:
                if i2 < n2 and (i1 >= n1 or i1 * n2 >= i2 * n1):
                    l2[i2](); i2 += 1
                else:
                    l1[i1](); i1 += 1

        def emit_flush_collective():
            o2 = P.op("pool", lambda E: E.collective_compute(
                "AllReduce", ALU.add, replica_groups=[list(range(NCORE))],
                ins=[cdum_in.ap().opt()], outs=[cdum_out.ap().opt()]),
                reads=[b_cdum], writes=[])
            o2.is_cc = True
            P.cc_ops.append(o2)
            P.cc_barrier(o2)

        with ExitStack() as p01:
            xt = [p01.enter_context(nc.sbuf_tensor(f"xt{i}", [128, CDIM], F32)) for i in range(2)]
            xnf = p01.enter_context(nc.sbuf_tensor("xnf", [128, CDIM], F32))
            xnb = [p01.enter_context(nc.sbuf_tensor(f"xnb{i}", [128, CDIM], BF16)) for i in range(2)]
            qi32 = p01.enter_context(nc.sbuf_tensor("qi32", [128, CDIM], I32))
            xqb = [p01.enter_context(nc.sbuf_tensor(f"xqb{i}", [128, CDIM], BF16)) for i in range(2)]
            srow = p01.enter_context(nc.sbuf_tensor("srow", [128, 1], F32))
            nmean = p01.enter_context(nc.sbuf_tensor("nmean", [128, 1], F32))
            sq = p01.enter_context(nc.sbuf_tensor("sq", [128, CDIM], F32))
            varr = p01.enter_context(nc.sbuf_tensor("varr", [128, 1], F32))
            stdr = p01.enter_context(nc.sbuf_tensor("stdr", [128, 1], F32))
            rstd = p01.enter_context(nc.sbuf_tensor("rstd", [128, 1], F32))
            nmrs = p01.enter_context(nc.sbuf_tensor("nmrs", [128, 1], F32))
            amax = p01.enter_context(nc.sbuf_tensor("amax", [128, 1], F32))
            rin = p01.enter_context(nc.sbuf_tensor("rin", [128, 1], F32))
            trp = [p01.enter_context(nc.psum_tensor(f"trp{i}", [128, 1024], BF16)) for i in range(2)]
            tstage = [p01.enter_context(nc.sbuf_tensor(f"tstage{i}", [128, 1024], BF16)) for i in range(2)]
            xqf = p01.enter_context(nc.sbuf_tensor("xqf", [128, NT * CDIM], BF16))
            wfb = [p01.enter_context(nc.sbuf_tensor(f"wfb{i}", [128, CDIM], BF16)) for i in range(NF)]
            wf8 = [p01.enter_context(nc.sbuf_tensor(f"wf8{i}", [128, CDIM], I8)) for i in range(2)]
            wsrow = p01.enter_context(nc.sbuf_tensor("wsrow", [1, CDIM], F32))
            brow = p01.enter_context(nc.sbuf_tensor("brow", [1, CDIM], F32))
            wsb = p01.enter_context(nc.sbuf_tensor("wsb", [128, CDIM], F32))
            biasb = p01.enter_context(nc.sbuf_tensor("biasb", [128, CDIM], F32))
            u1 = p01.enter_context(nc.sbuf_tensor("u1", [128, 512], F32))
            u2 = p01.enter_context(nc.sbuf_tensor("u2", [128, 512], F32))
            g16 = p01.enter_context(nc.sbuf_tensor("g16", [128, 512], FP16))
            gst = [p01.enter_context(nc.sbuf_tensor(f"gst{i}", [128, CDIM], FP16)) for i in range(2)]
            bc_ps = p01.enter_context(nc.psum_tensor("bc_ps", [128, 512], F32))
            ffn_ps = [p01.enter_context(nc.psum_tensor(f"ffn_ps{i}", [128, 512], F32)) for i in range(4)]

            b_xt = [Buf("xt0"), Buf("xt1")]
            b_xnb = [Buf("xnb0"), Buf("xnb1")]
            b_xqb = [Buf("xqb0"), Buf("xqb1")]
            b_scr = Buf("p0_scratch")
            b_trp = [Buf("trp0"), Buf("trp1")]
            b_tst = [Buf("tstage0"), Buf("tstage1")]
            b_xqf = [Buf(f"xqf{i}") for i in range(NT)]
            b_wfb = [Buf(f"wfb{i}") for i in range(NF)]
            b_wf8 = [Buf("wf8_0"), Buf("wf8_1")]
            b_rows = Buf("rows")
            b_bcast = Buf("bcast")
            b_bcps = Buf("bc_ps")
            b_u = Buf("ffn_u")
            b_ffnps = [Buf(f"ffn_ps{i}") for i in range(4)]
            b_gst = [Buf("gst0"), Buf("gst1")]

            def prep_thunks():
                th = []
                th.append(lambda: (
                    P.dma("sp", lambda E: E.dma_start(out=wsrow[:, :], in_=wscale_r.ap()), writes=[b_rows]),
                    P.dma("sp", lambda E: E.dma_start(out=brow[:, :], in_=bias_r.ap()), writes=[b_rows])))
                for fc in range(NF):
                    def w(fc=fc):
                        r = fc % 2
                        P.dma("pool", lambda E: E.dma_start(
                            out=wf8[r][:, :], in_=wffnT.ap()[fc * 128:(fc + 1) * 128, :]),
                            writes=[b_wf8[r]])
                        P.op("dve", lambda E: E.tensor_copy(wfb[fc][:, :], wf8[r][:, :]),
                             reads=[b_wf8[r]], writes=[b_wfb[fc]])
                    th.append(w)
                def bc():
                    for j in range(4):
                        P.op("pe", (lambda j: lambda E: E.matmul(bc_ps[:, :], onesr[:, :],
                                    wsrow[:, j * 512:(j + 1) * 512], start=True, stop=True))(j),
                             reads=[b_rows, b_consts], writes=[b_bcps])
                        P.op("act", (lambda j: lambda E: E.copy(out=wsb[:, j * 512:(j + 1) * 512],
                                    in_=bc_ps[:, :]))(j), reads=[b_bcps], writes=[b_bcast])
                    for j in range(4):
                        P.op("pe", (lambda j: lambda E: E.matmul(bc_ps[:, :], onesr[:, :],
                                    brow[:, j * 512:(j + 1) * 512], start=True, stop=True))(j),
                             reads=[b_rows, b_consts], writes=[b_bcps])
                        P.op("act", (lambda j: lambda E: E.copy(out=biasb[:, j * 512:(j + 1) * 512],
                                    in_=bc_ps[:, :]))(j), reads=[b_bcps], writes=[b_bcast])
                th.append(bc)
                return th

            xn_r = [xn_in[i].ap().rearrange("(f p) t -> p f t", p=128) for i in range(2)]

            def p0_thunks(tc):
                r = tc % 2
                bx = b_xt[r]
                th = []
                th.append(lambda: P.dma("sp", lambda E: E.dma_start(
                    out=xt[r][:, :], in_=x_own.ap()[tc * 128:(tc + 1) * 128, :]), writes=[bx]))
                def ln1():
                    P.op("dve", lambda E: E.reduce_sum(srow[:, :], xt[r][:, :], axis=AXX, negate=True),
                         reads=[bx], writes=[b_scr])
                    P.op("dve", lambda E: E.tensor_scalar_mul(nmean[:, :], srow[:, :], 1.0 / CDIM),
                         reads=[b_scr], writes=[b_scr])
                    P.op("act", lambda E: E.activation(out=sq[:, :], in_=xt[r][:, :], func=AF.Square,
                                                       bias=nmean[:, :], scale=1.0),
                         reads=[bx, b_scr], writes=[b_scr])
                th.append(ln1)
                def ln2():
                    P.op("dve", lambda E: E.reduce_sum(varr[:, :], sq[:, :], axis=AXX),
                         reads=[b_scr], writes=[b_scr])
                    P.op("dve", lambda E: E.tensor_scalar(varr[:, :], varr[:, :], 1.0 / CDIM, 1e-5, ALU.mult, ALU.add),
                         reads=[b_scr], writes=[b_scr])
                    P.op("act", lambda E: E.activation(out=stdr[:, :], in_=varr[:, :], func=AF.Sqrt),
                         reads=[b_scr], writes=[b_scr])
                    P.op("dve", lambda E: E.reciprocal(rstd[:, :], stdr[:, :]),
                         reads=[b_scr], writes=[b_scr])
                    P.op("dve", lambda E: E.tensor_mul(nmrs[:, :], nmean[:, :], rstd[:, :]),
                         reads=[b_scr], writes=[b_scr])
                    P.op("dve", lambda E: E.tensor_scalar(xnf[:, :], xt[r][:, :], rstd[:, :],
                                                          nmrs[:, :], ALU.mult, ALU.add),
                         reads=[bx, b_scr], writes=[b_scr])
                    P.op("act", lambda E: E.copy(out=xnb[r][:, :], in_=xnf[:, :]),
                         reads=[b_scr], writes=[b_xnb[r]])
                th.append(ln2)
                def qn():
                    P.op("dve", lambda E: E.reduce_max(amax[:, :], xnf[:, :], axis=AXX, apply_absolute_value=True),
                         reads=[b_scr], writes=[b_scr])
                    P.op("dve", lambda E: E.tensor_scalar_max(amax[:, :], amax[:, :], 1e-7),
                         reads=[b_scr], writes=[b_scr])
                    P.op("dve", lambda E: E.tensor_scalar_mul(iscale[:, tc:tc + 1], amax[:, :], 1.0 / 127.0),
                         reads=[b_scr], writes=[b_iscale])
                    P.op("dve", lambda E: E.reciprocal(rin[:, :], amax[:, :]),
                         reads=[b_scr], writes=[b_scr])
                    P.op("dve", lambda E: E.tensor_scalar_mul(rin[:, :], rin[:, :], 127.0),
                         reads=[b_scr], writes=[b_scr])
                    P.op("dve", lambda E: E.tensor_scalar_mul(sq[:, :], xnf[:, :], rin[:, :]),
                         reads=[b_scr], writes=[b_scr])
                    P.op("dve", lambda E: E.tensor_copy(qi32[:, :], sq[:, :]),
                         reads=[b_scr], writes=[b_scr])
                    P.op("dve", lambda E: E.tensor_copy(xqb[r][:, :], qi32[:, :]),
                         reads=[b_scr], writes=[b_xqb[r]])
                th.append(qn)
                for half in range(2):
                    def txn(half=half):
                        bt = b_trp[half]
                        bst = b_tst[half]
                        for j in range(8):
                            fc = half * 8 + j
                            P.op("pe", (lambda fc, j: lambda E: E.transpose(
                                trp[half][:, j * 128:(j + 1) * 128], xnb[r][:, fc * 128:(fc + 1) * 128],
                                identb[:, :]))(fc, j),
                                reads=[b_xnb[r], b_consts], writes=[bt])
                        P.op("act", lambda E: E.copy(out=tstage[half][:, :], in_=trp[half][:, :]),
                             reads=[bt], writes=[bst])
                        hf = tc // 4
                        tl = tc % 4
                        P.dma("pool", lambda E: E.dma_start(
                            out=xn_r[hf][:, half * 8:(half + 1) * 8, tl * 128:(tl + 1) * 128],
                            in_=tstage[half][:, :].rearrange("p (f t) -> p f t", t=128)),
                            reads=[bst], writes=[b_xn_in[hf]])
                    th.append(txn)
                for half in range(2):
                    def txq(half=half):
                        bt = b_trp[half]
                        for j in range(8):
                            fc = half * 8 + j
                            P.op("pe", (lambda fc, j: lambda E: E.transpose(
                                trp[half][:, j * 128:(j + 1) * 128], xqb[r][:, fc * 128:(fc + 1) * 128],
                                identb[:, :]))(fc, j),
                                reads=[b_xqb[r], b_consts], writes=[bt])
                        off = (tc * 16 + half * 8) * 128
                        P.op("act", lambda E: E.copy(out=xqf[:, off:off + 1024], in_=trp[half][:, :]),
                             reads=[bt], writes=[b_xqf[tc]])
                    th.append(txq)
                return th

            def ffn_thunks(tc):
                r = tc % 2
                th = []
                for ng in range(4):
                    def mm(ng=ng):
                        for fc in range(NF):
                            off = (tc * 16 + fc) * 128
                            P.op("pe", (lambda fc, ng, off: lambda E: E.matmul(
                                ffn_ps[ng][:, :], xqf[:, off:off + 128],
                                wfb[fc][:, ng * 512:(ng + 1) * 512],
                                start=(fc == 0), stop=(fc == NF - 1)))(fc, ng, off),
                                reads=[b_xqf[tc], b_wfb[fc]], writes=[b_ffnps[ng]])
                    th.append(mm)
                for ng in range(4):
                    def ep(ng=ng):
                        P.op("dve", lambda E: E.tensor_mul(u1[:, :], ffn_ps[ng][:, :],
                                                           wsb[:, ng * 512:(ng + 1) * 512]),
                             reads=[b_ffnps[ng], b_bcast], writes=[b_u])
                        P.op("dve", lambda E: E.tensor_scalar_mul(u1[:, :], u1[:, :], iscale[:, tc:tc + 1]),
                             reads=[b_u, b_iscale], writes=[b_u])
                        P.op("dve", lambda E: E.tensor_add(u2[:, :], u1[:, :],
                                                           biasb[:, ng * 512:(ng + 1) * 512]),
                             reads=[b_u, b_bcast], writes=[b_u])
                        P.op("act", lambda E: E.copy(out=g16[:, :], in_=u2[:, :]),
                             reads=[b_u], writes=[b_u])
                        P.op("act", lambda E: E.activation(out=gst[r][:, ng * 512:(ng + 1) * 512],
                                                           in_=g16[:, :], func=AF.Silu),
                             reads=[b_u], writes=[b_gst[r]])
                    th.append(ep)
                th.append(lambda: P.dma("pool", lambda E: E.dma_start(
                    out=g_sp.ap()[tc * 128:(tc + 1) * 128, :], in_=gst[r][:, :]),
                    reads=[b_gst[r]], writes=[b_g_sp]))
                return th

            EXP_SERIAL_FFN = False
            if EXP_SERIAL_FFN:
                for t in prep_thunks():
                    t()
                for tc in range(NT):
                    for t in p0_thunks(tc):
                        t()
                for tc in range(NT):
                    for t in ffn_thunks(tc):
                        t()
            else:
                pending = [prep_thunks(), []]
                for tc in range(NT):
                    merged_emit_generic(p0_thunks(tc), pending.pop(0))
                    pending.append(ffn_thunks(tc))
                    if tc == 3:
                        oa = P.op("pool", lambda E: E.collective_compute(
                            "AllGather", ALU.bypass, replica_groups=[list(range(NCORE))],
                            ins=[xn_in[0].ap().opt()], outs=[xn_gath[0].ap().opt()]),
                            reads=[b_xn_in[0]], writes=[b_xn_gath[0]])
                        oa.is_cc = True
                        P.cc_ops.append(oa)
                for lst in pending:
                    for t in lst:
                        t()
            o = P.op("pool", lambda E: E.collective_compute(
                "AllGather", ALU.bypass, replica_groups=[list(range(NCORE))],
                ins=[xn_in[1].ap().opt()], outs=[xn_gath[1].ap().opt()]),
                reads=[b_xn_in[1]], writes=[b_xn_gath[1]])
            o.is_cc = True
            P.cc_ops.append(o)
            P.barrier()

        p23 = ctx.enter_context(ExitStack())
        q_fm = p23.enter_context(nc.sbuf_tensor("q_fm", [128, HPC * L], BF16))
        k_fm = p23.enter_context(nc.sbuf_tensor("k_fm", [128, HPC * L], BF16))
        v_tm = p23.enter_context(nc.sbuf_tensor("v_tm", [128, HPC * L], BF16))

        with ExitStack() as p2:
            mt = [p2.enter_context(nc.sbuf_tensor(f"mt{i}", [128, 128], BF16)) for i in range(6)]
            mtf = p2.enter_context(nc.sbuf_tensor("mtf", [128, 128], F32))
            wraw = [p2.enter_context(nc.sbuf_tensor(f"wraw{i}", [128, CDIM], F32)) for i in range(2)]
            wrb = p2.enter_context(nc.sbuf_tensor("wrb", [128, CDIM], BF16))
            wTr = p2.enter_context(nc.sbuf_tensor("wTr", [128, 96 * 128], BF16))
            xng = [p2.enter_context(nc.sbuf_tensor(f"xng{i}", [128, NF * 512], BF16)) for i in range(2)]
            vstage = p2.enter_context(nc.sbuf_tensor("vstage", [128, 512], BF16))
            rope_ps = p2.enter_context(nc.psum_tensor("rope_ps", [128, 512], F32))
            vt_ps = p2.enter_context(nc.psum_tensor("vt_ps", [128, 1024], BF16))
            qkv_ps = [p2.enter_context(nc.psum_tensor(f"qkv_ps{i}", [128, 512], F32)) for i in range(6)]

            b_mt = Buf("mt")
            b_wraw = [Buf("wraw0"), Buf("wraw1")]
            b_wrb = Buf("wrb")
            b_wTr = Buf("wTr")
            b_ropeps = Buf("rope_ps")
            b_xng = [Buf("xng0"), Buf("xng1")]
            b_qkvps = [Buf(f"qkv_ps{i}") for i in range(6)]
            b_vstage = Buf("vstage")
            b_vtps = Buf("vt_ps")

            for jc in range(6):
                P.dma("sp", (lambda jc: lambda E: E.dma_start(
                    out=mtf[:, :], in_=mrot.ap()[jc * 128:(jc + 1) * 128, :]))(jc),
                    writes=[b_mt])
                P.op("dve", (lambda jc: lambda E: E.tensor_copy(mt[jc][:, :], mtf[:, :]))(jc),
                     reads=[b_mt], writes=[b_mt])
            for jc in range(6):
                r = jc % 2
                P.dma("sp", (lambda jc, r: lambda E: E.dma_start(
                    out=wraw[r][:, :], in_=w_qkv.ap()[jc * 128:(jc + 1) * 128, :]))(jc, r),
                    writes=[b_wraw[r]])
                P.op("dve", (lambda r: lambda E: E.tensor_copy(wrb[:, :], wraw[r][:, :]))(r),
                     reads=[b_wraw[r]], writes=[b_wrb])
                for fq in range(4):
                    for j in range(4):
                        fc = fq * 4 + j
                        P.op("pe", (lambda jc, fc, j: lambda E: E.matmul(
                            rope_ps[:, j * 128:(j + 1) * 128], wrb[:, fc * 128:(fc + 1) * 128],
                            mt[jc][:, :], start=True, stop=True))(jc, fc, j),
                            reads=[b_wrb, b_mt], writes=[b_ropeps])
                    for j in range(4):
                        fc = fq * 4 + j
                        off = (fc * 6 + jc) * 128
                        P.op("act", (lambda off, j: lambda E: E.copy(
                            out=wTr[:, off:off + 128], in_=rope_ps[:, j * 128:(j + 1) * 128]))(off, j),
                            reads=[b_ropeps], writes=[b_wTr])
            t16_order = [t for t in range(NS) if t % 2 == 0] + [t for t in range(NS) if t % 2 == 1]
            for idx, t16 in enumerate(t16_order):
                r = idx % 2
                rank, half = t16 // 2, t16 % 2
                P.dma("sp", (lambda rank, half, r: lambda E: E.dma_start(
                    out=xng[r][:, :].rearrange("p (f t) -> p f t", t=512),
                    in_=xn_gath[half].ap()[rank * CDIM:(rank + 1) * CDIM, :]
                        .rearrange("(f p) t -> p f t", p=128)))(rank, half, r),
                    reads=[b_xn_gath[half]], writes=[b_xng[r]])
                for jc in range(6):
                    for fc in range(NF):
                        off = (fc * 6 + jc) * 128
                        P.op("pe", (lambda jc, fc, off, r: lambda E: E.matmul(
                            qkv_ps[jc][:, :], wTr[:, off:off + 128], xng[r][:, fc * 512:(fc + 1) * 512],
                            start=(fc == 0), stop=(fc == NF - 1)))(jc, fc, off, r),
                            reads=[b_wTr, b_xng[r]], writes=[b_qkvps[jc]])
                for jc in range(4):
                    dst = q_fm if jc < 2 else k_fm
                    bdst = b_qfm if jc < 2 else b_kfm
                    h = jc % 2
                    P.op("act", (lambda jc, h, t16, dst: lambda E: E.copy(
                        out=dst[:, h * L + t16 * 512: h * L + (t16 + 1) * 512], in_=qkv_ps[jc][:, :]))(jc, h, t16, dst),
                        reads=[b_qkvps[jc]], writes=[bdst])
                for jc in range(4, 6):
                    h = jc - 4
                    P.op("act", (lambda jc: lambda E: E.copy(out=vstage[:, :], in_=qkv_ps[jc][:, :]))(jc),
                         reads=[b_qkvps[jc]], writes=[b_vstage])
                    for a in range(4):
                        P.op("pe", (lambda a: lambda E: E.transpose(
                            vt_ps[:, a * 128:(a + 1) * 128], vstage[:, a * 128:(a + 1) * 128],
                            identb[:, :]))(a),
                            reads=[b_vstage, b_consts], writes=[b_vtps])
                    P.op("act", (lambda h, t16: lambda E: E.copy(
                        out=v_tm[:, h * L + t16 * 512: h * L + (t16 + 1) * 512], in_=vt_ps[:, 0:512]))(h, t16),
                        reads=[b_vtps], writes=[b_vtm])
            if DEBUG:
                P.dma("pool", lambda E: E.dma_start(out=dbg_q.ap(), in_=q_fm[:, :]), reads=[b_qfm])
                P.dma("pool", lambda E: E.dma_start(out=dbg_k.ap(), in_=k_fm[:, :]), reads=[b_kfm])
                P.dma("pool", lambda E: E.dma_start(out=dbg_v.ap(), in_=v_tm[:, :]), reads=[b_vtm])
            P.barrier()

        wpb = [ctx.enter_context(nc.sbuf_tensor(f"wpb{i}", [128, CDIM], BF16, side="right")) for i in range(NF)]
        wpr = [ctx.enter_context(nc.sbuf_tensor(f"wpr{i}", [128, CDIM], F32, side="right")) for i in range(2)]
        b_wpb = [Buf(f"wpb{i}") for i in range(NF)]
        b_wpr = [Buf("wpr0"), Buf("wpr1")]
        for cc in range(NF):
            r = cc % 2
            P.dma("sp", (lambda cc, r: lambda E: E.dma_start(
                out=wpr[r][:, :], in_=wprojT.ap()[cc * 128:(cc + 1) * 128, :]))(cc, r),
                writes=[b_wpr[r]])
            P.op("dve", (lambda cc, r: lambda E: E.tensor_copy(wpb[cc][:, :], wpr[r][:, :]))(cc, r),
                 reads=[b_wpr[r]], writes=[b_wpb[cc]])

        with ExitStack() as p3:
            m_col = [p3.enter_context(nc.sbuf_tensor(f"m_col{i}", [128, 4], F32)) for i in range(2)]
            mneg = [p3.enter_context(nc.sbuf_tensor(f"mneg{i}", [1, 512], BF16)) for i in range(2)]
            mbneg = [p3.enter_context(nc.sbuf_tensor(f"mbneg{i}", [128, 512], BF16)) for i in range(2)]
            pT = [p3.enter_context(nc.sbuf_tensor(f"pT{i}", [128, 512], BF16)) for i in range(4)]
            bm = [p3.enter_context(nc.sbuf_tensor(f"bm{i}", [128, 1], F32)) for i in range(2)]
            l_sb = p3.enter_context(nc.sbuf_tensor("l_sb", [1, 512], F32))
            lr_sb = p3.enter_context(nc.sbuf_tensor("lr_sb", [1, 512], F32))
            bcf = p3.enter_context(nc.sbuf_tensor("bcf", [128, 512], F32))
            ofin = [p3.enter_context(nc.sbuf_tensor(f"ofin{i}", [128, 512], BF16)) for i in range(2)]
            s1_ps = [p3.enter_context(nc.psum_tensor("s1_ps0", [128, 512], F32))]
            sT_ps = [p3.enter_context(nc.psum_tensor(f"sT_ps{i}", [128, 512], F32)) for i in range(3)]
            o_ps = p3.enter_context(nc.psum_tensor("o_ps", [128, 512], F32))
            l_ps = p3.enter_context(nc.psum_tensor("l_ps", [1, 512], F32))
            mtr_ps = p3.enter_context(nc.psum_tensor("mtr_ps", [1, 512], F32))
            bc2_ps = p3.enter_context(nc.psum_tensor("bc2_ps", [128, 512], F32))

            b_mcol = [Buf("m_col0"), Buf("m_col1")]
            b_mneg = [Buf("mneg0"), Buf("mneg1")]
            b_mbneg = [Buf("mbneg0"), Buf("mbneg1")]
            b_pT = [Buf(f"pT{i}") for i in range(4)]
            b_bm = [Buf("bm0"), Buf("bm1")]
            b_lsb = Buf("l_sb"); b_bcf = Buf("bcf")
            b_ofin = [Buf("ofin0"), Buf("ofin1")]
            b_s1 = [Buf("s1_ps0")]
            b_sT = [Buf(f"sT_ps{i}") for i in range(3)]
            b_ops = Buf("o_ps"); b_lps = Buf("l_ps")
            b_mtr = Buf("mtr_ps"); b_bc2 = Buf("bc2_ps")

            def pass1_thunks(s, h, pp):
                hb = h * L
                th = []
                th.append(lambda: P.op("dve", lambda E: E.memset(m_col[pp][:, :], NEG_BIG),
                                       writes=[b_mcol[pp]]))
                items = [(kb, qi) for kb in range(s + 1) for qi in range(4)]
                for idx, (kb, qi) in enumerate(items):
                    rot = 0
                    width = (qi + 1) * 128 if kb == s else 512

                    def stats_tile(kb=kb, qi=qi, rot=rot, width=width, hb=hb, s=s, pp=pp):
                        bs = b_s1[rot]
                        P.op("pe", lambda E: E.matmul(
                            s1_ps[rot][:, 0:width],
                            q_fm[:, hb + s * 512 + qi * 128: hb + s * 512 + (qi + 1) * 128],
                            k_fm[:, hb + kb * 512: hb + kb * 512 + width],
                            start=True, stop=True),
                            reads=[b_qfm, b_kfm], writes=[bs])
                        if kb == s:
                            P.op("dve", lambda E: E.tensor_add(
                                s1_ps[rot][:, qi * 128:(qi + 1) * 128],
                                s1_ps[rot][:, qi * 128:(qi + 1) * 128], maskA[:, :]),
                                reads=[bs, b_consts], writes=[bs])
                        P.op("dve", lambda E: E.reduce_max(
                            bm[pp][:, :], s1_ps[rot][:, 0:width], axis=AXX),
                            reads=[bs], writes=[b_bm[pp]])
                        P.op("dve", lambda E: E.tensor_max(
                            m_col[pp][:, qi:qi + 1], m_col[pp][:, qi:qi + 1], bm[pp][:, :]),
                            reads=[b_bm[pp], b_mcol[pp]], writes=[b_mcol[pp]])
                    th.append(stats_tile)

                def fin(pp=pp):
                    P.op("dve", lambda E: E.tensor_scalar_max(m_col[pp][:, :], m_col[pp][:, :], M_CLAMP),
                         reads=[b_mcol[pp]], writes=[b_mcol[pp]])
                    for qi in range(4):
                        P.op("pe", (lambda qi: lambda E: E.transpose(
                            mtr_ps[:, qi * 128:(qi + 1) * 128], m_col[pp][:, qi:qi + 1], identf[:, :]))(qi),
                            reads=[b_mcol[pp], b_consts], writes=[b_mtr])
                    P.op("act", lambda E: E.activation(out=mneg[pp][:, :], in_=mtr_ps[0:1, :], func=AF.Copy,
                                                       bias=0.0, scale=-1.0),
                         reads=[b_mtr], writes=[b_mneg[pp]])
                    P.op("pe", lambda E: E.matmul(bc2_ps[:, :], onesrb[:, :], mneg[pp][:, :],
                                                  start=True, stop=True),
                         reads=[b_mneg[pp], b_consts], writes=[b_bc2])
                    P.op("act", lambda E: E.copy(out=mbneg[pp][:, :], in_=bc2_ps[:, :]),
                         reads=[b_bc2], writes=[b_mbneg[pp]])
                th.append(fin)
                return th

            def pass2_thunks(s, h, pp):
                hb = h * L
                n_iter = (s + 1) * 4
                th = []
                items = [(kb, a) for kb in range(s + 1) for a in range(4)]

                def front(kb, a, rot, use_pe_shift):
                    bs = b_sT[rot]
                    bp = b_pT[a]
                    q0 = a * 128 if kb == s else 0
                    if use_pe_shift:
                        P.op("pe", lambda E: E.matmul(
                            sT_ps[rot][:, :], onesrb[:, :], mneg[pp][:, :], start=True, stop=False),
                            reads=[b_mneg[pp], b_consts], writes=[bs])
                        P.op("pe", lambda E: E.matmul(
                            sT_ps[rot][:, q0:512],
                            k_fm[:, hb + kb * 512 + a * 128: hb + kb * 512 + (a + 1) * 128],
                            q_fm[:, hb + s * 512 + q0: hb + (s + 1) * 512],
                            start=False, stop=True),
                            reads=[b_kfm, b_qfm], writes=[bs])
                    else:
                        P.op("pe", lambda E: E.matmul(
                            sT_ps[rot][:, q0:512],
                            k_fm[:, hb + kb * 512 + a * 128: hb + kb * 512 + (a + 1) * 128],
                            q_fm[:, hb + s * 512 + q0: hb + (s + 1) * 512],
                            start=True, stop=True),
                            reads=[b_kfm, b_qfm], writes=[bs])
                        if q0 > 0:
                            P.op("dve", lambda E: E.memset(sT_ps[rot][:, 0:q0], 0.0),
                                 reads=[bs], writes=[bs])
                        P.op("dve", lambda E: E.tensor_add(
                            sT_ps[rot][:, :], sT_ps[rot][:, :], mbneg[pp][:, :]),
                            reads=[bs, b_mbneg[pp]], writes=[bs])
                    if kb == s:
                        P.op("dve", lambda E: E.tensor_add(
                            sT_ps[rot][:, a * 128:(a + 1) * 128],
                            sT_ps[rot][:, a * 128:(a + 1) * 128], maskB[:, :]),
                            reads=[bs, b_consts], writes=[bs])
                    P.op("act", lambda E: E.activation(
                        out=pT[a][:, :], in_=sT_ps[rot][:, :], func=AF.Exp),
                        reads=[bs], writes=[bp])
                    if kb == s and a > 0:
                        P.op("dve", lambda E: E.memset(pT[a][:, 0:a * 128], 0.0),
                             reads=[], writes=[bp])

                def back(kb, a, first, last):
                    bp = b_pT[a]
                    P.op("pe", lambda E: E.matmul(
                        o_ps[:, :], v_tm[:, hb + (kb * 4 + a) * 128: hb + (kb * 4 + a + 1) * 128],
                        pT[a][:, :], start=first, stop=last),
                        reads=[b_vtm, bp], writes=[b_ops])
                    P.op("pe", lambda E: E.matmul(
                        l_ps[:, :], onescb[:, :], pT[a][:, :], start=first, stop=last),
                        reads=[b_consts, bp], writes=[b_lps])

                for it, (kb, a) in enumerate(items):
                    def step(it=it, kb=kb, a=a):
                        front(kb, a, it % 3, it % 2 == 0)
                        if it > 1:
                            pk, pa = items[it - 2]
                            back(pk, pa, it - 2 == 0, False)
                    th.append(step)

                def fin(s=s, h=h):
                    for it in range(max(0, n_iter - 2), n_iter):
                        pk, pa = items[it]
                        back(pk, pa, it == 0, it == n_iter - 1)
                    P.op("dve", lambda E: E.tensor_scalar_add(l_sb[:, :], l_ps[0:1, :], 1e-30),
                         reads=[b_lps], writes=[b_lsb])
                    P.op("act", lambda E: E.activation(out=l_sb[:, :], in_=l_sb[:, :], func=AF.Ln),
                         reads=[b_lsb], writes=[b_lsb])
                    P.op("act", lambda E: E.activation(out=lr_sb[:, :], in_=l_sb[:, :], func=AF.Exp,
                                                       bias=0.0, scale=-1.0),
                         reads=[b_lsb], writes=[b_lsb])
                    P.op("pe", lambda E: E.matmul(bc2_ps[:, :], onesr[:, :], lr_sb[:, :], start=True, stop=True),
                         reads=[b_lsb, b_consts], writes=[b_bc2])
                    P.op("act", lambda E: E.copy(out=bcf[:, :], in_=bc2_ps[:, :]),
                         reads=[b_bc2], writes=[b_bcf])
                    fr = (s * 2 + h) % 2
                    P.op("dve", lambda E: E.tensor_mul(ofin[fr][:, :], o_ps[:, :], bcf[:, :]),
                         reads=[b_ops, b_bcf], writes=[b_ofin[fr]])
                    P.dma("pool", lambda E: E.dma_start(
                        out=a2a_in.ap()[(s // 2) * 256 + h * 128: (s // 2) * 256 + (h + 1) * 128,
                                        (s % 2) * 512:((s % 2) + 1) * 512],
                        in_=ofin[fr][:, :]),
                        reads=[b_ofin[fr]], writes=[b_a2a_in])
                th.append(fin)
                return th

            def merged_emit(l1, l2):
                n1, n2 = len(l1), len(l2)
                i1 = i2 = 0
                while i1 < n1 or i2 < n2:
                    if i2 < n2 and (i1 >= n1 or i1 * n2 >= i2 * n1):
                        l2[i2](); i2 += 1
                    else:
                        l1[i1](); i1 += 1

            iters = [(s, h) for s in range(NS) for h in range(HPC)]
            for t in pass1_thunks(*iters[0], 0):
                t()
            for i in range(len(iters)):
                nxt = pass1_thunks(*iters[i + 1], (i + 1) % 2) if i + 1 < len(iters) else []
                merged_emit(nxt, pass2_thunks(*iters[i], i % 2))

            def cc2(E):
                with allow_ici_all_to_all():
                    return E.collective_compute(
                        "AllToAll", ALU.bypass, replica_groups=[list(range(NCORE))],
                        ins=[a2a_in.ap().opt()], outs=[a2a_out.ap().opt()])
            o = P.op("pool", cc2, reads=[b_a2a_in], writes=[b_a2a_out])
            o.is_cc = True
            P.cc_ops.append(o)
            P.barrier()
        p23.close()

        with ExitStack() as p5:
            oat_t = p5.enter_context(nc.sbuf_tensor("oat", [128, NF * TOWN], BF16))
            xt2 = [p5.enter_context(nc.sbuf_tensor(f"xt2_{i}", [128, CDIM], F32)) for i in range(2)]
            gt = [p5.enter_context(nc.sbuf_tensor(f"gt{i}", [128, CDIM], FP16)) for i in range(2)]
            gt32 = p5.enter_context(nc.sbuf_tensor("gt32", [128, 512], F32))
            r1 = p5.enter_context(nc.sbuf_tensor("r1", [128, 512], F32))
            outst = [p5.enter_context(nc.sbuf_tensor(f"outst{i}", [128, CDIM], F32)) for i in range(2)]
            op_ps = [p5.enter_context(nc.psum_tensor(f"op_ps{i}", [128, 512], F32)) for i in range(8)]

            b_oat = Buf("oat")
            b_xt2 = [Buf("xt2_0"), Buf("xt2_1")]
            b_gt = [Buf("gt0"), Buf("gt1")]
            b_r = Buf("p5_scratch")
            b_opps = [Buf(f"op_ps{i}") for i in range(8)]
            b_outst = [Buf("outst0"), Buf("outst1")]

            P.dma("sp", lambda E: E.dma_start(
                out=oat_t[:, :].rearrange("p (f t) -> p f t", t=TOWN),
                in_=a2a_out.ap().rearrange("(f p) t -> p f t", p=128)),
                reads=[b_a2a_out], writes=[b_oat])
            for tc in range(NT):
                r = tc % 2
                P.dma("sp", (lambda tc, r: lambda E: E.dma_start(
                    out=xt2[r][:, :], in_=x_own.ap()[tc * 128:(tc + 1) * 128, :]))(tc, r),
                    writes=[b_xt2[r]])
                P.dma("sp", (lambda tc, r: lambda E: E.dma_start(
                    out=gt[r][:, :], in_=g_sp.ap()[tc * 128:(tc + 1) * 128, :]))(tc, r),
                    reads=[b_g_sp], writes=[b_gt[r]])
                for ng in range(4):
                    pi = (tc % 2) * 4 + ng
                    for cc in range(NF):
                        P.op("pe", (lambda tc, ng, cc, pi: lambda E: E.matmul(
                            op_ps[pi][:, :], oat_t[:, cc * TOWN + tc * 128: cc * TOWN + (tc + 1) * 128],
                            wpb[cc][:, ng * 512:(ng + 1) * 512],
                            start=(cc == 0), stop=(cc == NF - 1)))(tc, ng, cc, pi),
                            reads=[b_oat, b_wpb[cc]], writes=[b_opps[pi]])
                for ng in range(4):
                    pi = (tc % 2) * 4 + ng
                    P.op("act", (lambda r, ng: lambda E: E.copy(
                        out=gt32[:, :], in_=gt[r][:, ng * 512:(ng + 1) * 512]))(r, ng),
                        reads=[b_gt[r]], writes=[b_r])
                    P.op("dve", (lambda pi: lambda E: E.tensor_mul(r1[:, :], op_ps[pi][:, :], gt32[:, :]))(pi),
                         reads=[b_opps[pi], b_r], writes=[b_r])
                    P.op("dve", (lambda r, ng: lambda E: E.tensor_add(
                        outst[r][:, ng * 512:(ng + 1) * 512], r1[:, :],
                        xt2[r][:, ng * 512:(ng + 1) * 512]))(r, ng),
                         reads=[b_r, b_xt2[r]], writes=[b_outst[r]])
                P.dma("pool", (lambda tc, r: lambda E: E.dma_start(
                    out=out_c.ap()[tc * 128:(tc + 1) * 128, :], in_=outst[r][:, :]))(tc, r),
                    reads=[b_outst[r]], writes=[b_out])
            if DEBUG:
                P.dma("pool", lambda E: E.dma_start(out=dbg_g.ap(), in_=g_sp.ap()),
                      reads=[b_g_sp])
                P.dma("pool", lambda E: E.dma_start(out=dbg_ai.ap(), in_=a2a_in.ap()),
                      reads=[b_a2a_in])
                P.dma("pool", lambda E: E.dma_start(out=dbg_ao.ap(), in_=a2a_out.ap()),
                      reads=[b_a2a_out])
            P.barrier()

        sems, runner = P.materialize(nc, ctx)
        block = ctx.enter_context(nc.Block())
        block.tensor(runner("pe"))
        block.scalar(runner("act"))
        block.vector(runner("dve"))
        block.gpsimd(runner("pool"))
        block.sync(runner("sp"))

    return nc


_NC_CACHE = None


def kernel(L0_attn_x, L0_attn_w_attn, L0_attn_w_proj, L0_ffn_weight, L0_ffn_w_scale, L0_ffn_bias):
    global _NC_CACHE
    x = np.asarray(L0_attn_x)
    w_attn = np.asarray(L0_attn_w_attn)
    w_proj = np.asarray(L0_attn_w_proj)
    w_ffn = np.asarray(L0_ffn_weight)
    w_scale = np.asarray(L0_ffn_w_scale)
    bias = np.asarray(L0_ffn_bias)

    wprojT_np = np.ascontiguousarray(w_proj.T.astype(np.float32))
    wffnT_np = np.ascontiguousarray(w_ffn.T)
    wscale_np = np.ascontiguousarray(w_scale.reshape(1, CDIM).astype(np.float32))
    bias_np = np.ascontiguousarray(bias.reshape(1, CDIM).astype(np.float32))

    if _NC_CACHE is None:
        _NC_CACHE = build_nc()
    nc = _NC_CACHE

    in_maps = []
    for c in range(NCORE):
        rows = slice(c * 256, (c + 1) * 256)
        w_qkv_own = np.ascontiguousarray(np.concatenate(
            [w_attn[rows], w_attn[CDIM + rows.start: CDIM + rows.stop],
             w_attn[2 * CDIM + rows.start: 2 * CDIM + rows.stop]], axis=0).astype(np.float32))
        in_maps.append({
            "x_own": np.ascontiguousarray(x[0, c * TOWN:(c + 1) * TOWN, :].astype(np.float32)),
            "w_qkv": w_qkv_own,
            "mrot": _m_blocks(c),
            "wprojT": wprojT_np,
            "wffnT": wffnT_np,
            "wscale_r": wscale_np,
            "bias_r": bias_np,
        })
    res = run_bass_kernel_spmd(nc, in_maps, core_ids=list(range(NCORE)))
    globals()["LAST_RES"] = res
    out = np.empty((1, L, CDIM), dtype=np.float32)
    for c in range(NCORE):
        out[0, c * TOWN:(c + 1) * TOWN, :] = res.results[c]["out_c"]
    return out
